# version 17
# baseline (speedup 1.0000x reference)
import math
import numpy as np
import ml_dtypes
import concourse.bass as bass
import concourse.mybir as mybir
from concourse.bass_utils import run_bass_kernel_spmd

F32 = mybir.dt.float32
BF16 = mybir.dt.bfloat16
AF = mybir.ActivationFunctionType
ALU = mybir.AluOpType
AX = mybir.AxisListType

D = 2048
NH = 16
HD = 128
TOK = 2048
HALO = 128
NK = D // 128
SCALE = 1.0 / math.sqrt(HD)
NEG = -1.0e38
EPS = 1e-5
NE = 8
HID = 1024
SH = 2048
CAP = 640


class Sched:

    def __init__(self, nc, ctx):
        self.nc = nc
        self.ctx = ctx
        self.ops = {k: [] for k in ("t", "v", "s", "y", "g")}
        self.cnt = dict(t=0, v=0, s=0)
        self.dslot = {}
        self.last_wait = {q: {} for q in self.ops}
        self.sems = {}
        for ck in ("t", "v", "s"):
            self.sems[ck] = ctx.enter_context(nc.semaphore("sem_" + ck))

    def _dma_sem(self, slot):
        if slot not in self.sems:
            self.sems[slot] = self.ctx.enter_context(
                self.nc.semaphore("sem_" + slot))
            self.dslot[slot] = 0
        return self.sems[slot]

    def _waits(self, q, deps):
        ws = []
        for ck, val in deps:
            if val is None or val <= 0:
                continue
            if ck == "t" and q == "t":
                continue
            if val > self.last_wait[q].get(ck, 0):
                self.last_wait[q][ck] = val
                step = 1 if ck in ("t", "v", "s") else 16
                if ck not in self.sems:
                    self._dma_sem(ck)
                ws.append((self.sems[ck], val * step))
        return ws

    def op(self, q, fn, deps=(), inc=True):
        assert q in ("t", "v", "s")
        ws = self._waits(q, deps)
        sem = None
        if inc:
            self.cnt[q] += 1
            sem = self.sems[q]

        def emit(eng, ws=ws, fn=fn, sem=sem):
            for s_, v_ in ws:
                eng.wait_ge(s_, v_)
            inst = fn(eng)
            if sem is not None:
                inst.then_inc(sem, 1)

        self.ops[q].append(emit)
        return self.cnt[q]

    def dma(self, q, slot, fn, deps=()):
        sem = self._dma_sem(slot)
        ws = self._waits(q, deps)
        self.dslot[slot] += 1

        def emit(eng, ws=ws, fn=fn, sem=sem):
            for s_, v_ in ws:
                eng.wait_ge(s_, v_)
            fn(eng).then_inc(sem, 16)

        self.ops[q].append(emit)
        return self.dslot[slot]

    def wait_only(self, q, deps):
        ws = self._waits(q, deps)
        if not ws:
            return

        def emit(eng, ws=ws):
            for s_, v_ in ws:
                eng.wait_ge(s_, v_)

        self.ops[q].append(emit)

    def barrier(self):
        snap = dict(self.cnt)
        snap.update(self.dslot)
        for q in self.ops:
            deps = [(ck, v) for ck, v in snap.items() if v > 0 and ck != q]
            self.wait_only(q, deps)

    def flush(self):
        with self.nc.Block() as block:
            for q, eng_name in (("t", "tensor"), ("v", "vector"),
                                ("s", "scalar"), ("y", "sync"), ("g", "gpsimd")):
                ops = self.ops[q]
                if not ops:
                    continue
                getattr(block, eng_name)(lambda e, ops=ops: [f(e) for f in ops])
        self.ops = {k: [] for k in self.ops}


class Banks:

    def __init__(self, tensors):
        self.t = tensors
        self.free_dep = [None] * len(tensors)
        self.i = 0

    def acquire(self):
        j = self.i % len(self.t)
        self.i += 1
        return j, self.t[j], self.free_dep[j]

    def consumed(self, j, ck, count):
        self.free_dep[j] = (ck, count)


def _ln_a(sc, xt_ap, st, deps):
    v1 = sc.op("v", lambda e: e.reduce_sum(st["ssum"], xt_ap, axis=AX.X),
               deps=deps + st["war"])
    s1 = sc.op("s", lambda e: e.activation(st["sq"], xt_ap, AF.Square,
                                           accum_out=st["ssq"]),
               deps=deps + st["war"] + [("s", sc.cnt["s"])])
    st["a"] = (v1, s1)


def _ln_b(sc, xt_ap, st, out_bf_ap, out_deps=()):
    v1, s1 = st["a"]
    ssum, ssq, mean, m2, var, rstd = (st["ssum"], st["ssq"], st["mean"],
                                      st["m2"], st["var"], st["rstd"])
    v3 = sc.op("v", lambda e: e.tensor_scalar(mean, ssum, 1.0 / D, None, ALU.mult),
               deps=[("v", v1)])
    v5 = sc.op("v", lambda e: e.tensor_mul(m2, mean, mean), deps=[("v", v3)])
    v6 = sc.op("v", lambda e: e.scalar_tensor_tensor(var, ssq, 1.0 / D, m2,
                                                     ALU.mult, ALU.subtract),
               deps=[("s", s1), ("v", v5)])
    v7 = sc.op("v", lambda e: e.tensor_scalar(var, var, EPS, None, ALU.add),
               deps=[("v", v6)])
    s2 = sc.op("s", lambda e: e.activation(var, var, AF.Ln), deps=[("v", v7)])
    s3 = sc.op("s", lambda e: e.activation(rstd, var, AF.Exp, scale=-0.5),
               deps=[("s", s2)])
    s4 = sc.op("v", lambda e: e.tensor_scalar(out_bf_ap, xt_ap, mean, rstd,
                                              ALU.subtract, ALU.mult),
               deps=[("s", s3)] + list(out_deps))
    st["war"] = [("v", s4), ("s", s3)]
    return s4


def build_program(debug=False):
    nc = bass.Bass("TRN2", target_bir_lowering=False)

    xh = nc.dram_tensor("xh", [HALO + TOK, D], F32, kind="ExternalInput").ap()
    wq = nc.dram_tensor("wq", [D, D], F32, kind="ExternalInput").ap()
    wk = nc.dram_tensor("wk", [D, D], F32, kind="ExternalInput").ap()
    wv = nc.dram_tensor("wv", [D, D], F32, kind="ExternalInput").ap()
    wo = nc.dram_tensor("wo", [D, D], F32, kind="ExternalInput").ap()
    bo = nc.dram_tensor("bo", [1, D], F32, kind="ExternalInput").ap()
    cost = nc.dram_tensor("cost", [128, HALO + TOK], F32, kind="ExternalInput").ap()
    sint = nc.dram_tensor("sint", [128, HALO + TOK], F32, kind="ExternalInput").ap()
    rt = nc.dram_tensor("rt", [128, 128], BF16, kind="ExternalInput").ap()
    idm = nc.dram_tensor("idm", [128, 128], BF16, kind="ExternalInput").ap()
    idmf = nc.dram_tensor("idmf", [128, 128], F32, kind="ExternalInput").ap()
    onesb = nc.dram_tensor("onesb", [1, 128], BF16, kind="ExternalInput").ap()
    maskA = nc.dram_tensor("maskA", [128, 256], F32, kind="ExternalInput").ap()
    maskB = nc.dram_tensor("maskB", [128, 256], F32, kind="ExternalInput").ap()
    gw = nc.dram_tensor("gw", [D, NE], F32, kind="ExternalInput").ap()
    b1t = nc.dram_tensor("b1t", [128, NE * 8], F32, kind="ExternalInput").ap()
    b2 = nc.dram_tensor("b2", [NE, D], F32, kind="ExternalInput").ap()
    w1 = nc.dram_tensor("w1", [NE, D, HID], F32, kind="ExternalInput").ap()
    w2 = nc.dram_tensor("w2", [NE, HID, D], F32, kind="ExternalInput").ap()
    sw1 = nc.dram_tensor("sw1", [D, SH], F32, kind="ExternalInput").ap()
    sb1t = nc.dram_tensor("sb1t", [128, 16], F32, kind="ExternalInput").ap()
    sw2 = nc.dram_tensor("sw2", [SH, D], F32, kind="ExternalInput").ap()
    sb2 = nc.dram_tensor("sb2", [1, D], F32, kind="ExternalInput").ap()
    iotap1 = nc.dram_tensor("iotap1", [128, CAP], F32, kind="ExternalInput").ap()
    tokv = nc.dram_tensor("tokv", [128, 16], F32, kind="ExternalInput").ap()

    x1s = nc.dram_tensor("x1s", [TOK, D], F32, kind="Internal").ap()
    l2s = nc.dram_tensor("l2s", [D, TOK], BF16, kind="Internal").ap()
    l2r = nc.dram_tensor("l2r", [TOK, D], BF16, kind="Internal").ap()
    l2rg = nc.dram_tensor("l2rg", [2, CAP, D], BF16, kind="Internal").ap()
    w1s = nc.dram_tensor("w1s", [NE, D, HID], BF16, kind="Internal").ap()
    w2s = nc.dram_tensor("w2s", [NE, HID, D], BF16, kind="Internal").ap()
    out = nc.dram_tensor("out", [TOK, D], F32, kind="ExternalOutput").ap()
    if debug:
        dbg_x1 = nc.dram_tensor("dbg_x1", [TOK, D], F32, kind="ExternalOutput").ap()
        dbg_l2 = nc.dram_tensor("dbg_l2", [D, TOK], BF16, kind="ExternalOutput").ap()
        dbg_g = nc.dram_tensor("dbg_g", [128, 16 * NE], F32, kind="ExternalOutput").ap()

    from contextlib import ExitStack
    with ExitStack() as ctx:
        sc = Sched(nc, ctx)

        ob = ctx.enter_context
        rt_b = ob(nc.sbuf_tensor("rt_b", [128, 128], BF16))
        idm_b = ob(nc.sbuf_tensor("idm_b", [128, 128], BF16))
        idmf_b = ob(nc.sbuf_tensor("idmf_b", [128, 128], F32))
        ones_b = ob(nc.sbuf_tensor("ones_b", [1, 128], BF16))
        mA_b = ob(nc.sbuf_tensor("mA_b", [128, 256], F32))
        mB_b = ob(nc.sbuf_tensor("mB_b", [128, 256], F32))
        bo_b = ob(nc.sbuf_tensor("bo_b", [1, D], BF16))
        gw_b = ob(nc.sbuf_tensor("gw_b", [128, NK, NE], BF16))
        b1t_b = ob(nc.sbuf_tensor("b1t_b", [128, NE * 8], F32))
        sb1t_b = ob(nc.sbuf_tensor("sb1t_b", [128, 16], F32))
        b2_b = ob(nc.sbuf_tensor("b2_b", [NE, D], BF16))
        sb2_b = ob(nc.sbuf_tensor("sb2_b", [1, D], BF16))
        gates = ob(nc.sbuf_tensor("gates", [128, 16 * NE], F32))
        gatesT = ob(nc.sbuf_tensor("gatesT", [NE, 16 * 128], BF16))
        idxT = ob(nc.sbuf_tensor("idxT", [128, NE * 5, 4], F32))
        idx_i = ob(nc.sbuf_tensor("idx_i", [128, NE * 5], mybir.dt.int32))


        g_sb = [ob(nc.sbuf_tensor("g_sb%d" % i, [128, D], BF16)) for i in range(2)]
        pf = [ob(nc.psum_tensor("pf%d" % i, [128, 512], F32)) for i in range(6)]
        pb = [ob(nc.psum_tensor("pb%d" % i, [128, 8, 128], BF16)) for i in range(2)]
        fbank = Banks(pf)
        bbank = Banks(pb)

        sc.dma("y", "yc", lambda e: e.dma_start(rt_b[:, :], rt))
        sc.dma("y", "yc", lambda e: e.dma_start(idm_b[:, :], idm))
        sc.dma("y", "yc", lambda e: e.dma_start(idmf_b[:, :], idmf))
        sc.dma("y", "yc", lambda e: e.dma_start(ones_b[:, :], onesb))
        sc.dma("y", "yc", lambda e: e.dma_start(mA_b[:, :], maskA))
        sc.dma("y", "yc", lambda e: e.dma_start(mB_b[:, :], maskB))
        sc.dma("y", "yc", lambda e: e.dma_start(b1t_b[:, :], b1t))
        ld_const_y = sc.dma("y", "yc", lambda e: e.dma_start(sb1t_b[:, :], sb1t))
        sc.dma("g", "gc", lambda e: e.dma_start(bo_b[:, :], bo))
        sc.dma("g", "gc", lambda e: e.dma_start(
            gw_b[:, :, :], gw.rearrange("(k p) e -> p k e", p=128)))
        sc.dma("g", "gc", lambda e: e.dma_start(b2_b[:, :], b2))
        ld_const_g = sc.dma("g", "gc", lambda e: e.dma_start(sb2_b[:, :], sb2))

        x1_store = {}
        l2_store_last = 0

        conv_jobs = []
        for cei in range(NE):
            conv_jobs.append((w1s[cei, :, :], w1[cei, :, :], cei))
            conv_jobs.append((w2s[cei, :, :], w2[cei, :, :], cei))
        conv_state = {"i": 0}

        def emit_conv(n=1):
            for _ in range(n):
                if conv_state["i"] >= len(conv_jobs):
                    return
                dstap, srcap, cei = conv_jobs[conv_state["i"]]
                conv_state["i"] += 1
                sc.dma("g", "wcv%d" % cei, lambda e, o=dstap, i=srcap:
                       e.dma_start(o, i), deps=[])

        abw = ExitStack()
        cos_b = abw.enter_context(nc.sbuf_tensor("cos_b", [128, HALO + TOK], BF16))
        sin_b = abw.enter_context(nc.sbuf_tensor("sin_b", [128, HALO + TOK], BF16))
        sc.dma("g", "gc", lambda e: e.dma_start(cos_b[:, :], cost))
        ld_const_g = sc.dma("g", "gc", lambda e: e.dma_start(sin_b[:, :], sint))
        x1_store = {}
        for sp in range(2):
            from contextlib import ExitStack as _ES
            with _ES() as spsc:
                a = spsc.enter_context
                lnxT = a(nc.sbuf_tensor("lnxT%d" % sp, [128, NK, HALO + 1024], BF16))
                attnT = a(nc.sbuf_tensor("attnT%d" % sp, [128, NH, 1024], BF16))
                base = sp * 1024

                with _ES() as asc:
                    aa = asc.enter_context
                    xt = [aa(nc.sbuf_tensor("xt%d_%d" % (sp, i), [128, D], F32)) for i in range(3)]
                    sqs = aa(nc.sbuf_tensor("sqs%d" % sp, [128, D], F32))
                    lnb_bufs = [aa(nc.sbuf_tensor("lnb%d_%d" % (sp, i), [128, D], BF16)) for i in range(2)]
                    stats = {}
                    for nm in ("ssum", "ssq", "mean", "m2", "var", "rstd"):
                        stats[nm] = [aa(nc.sbuf_tensor("stA%d_%s%d" % (sp, nm, i), [128, 1], F32))
                                     for i in range(3)]
                    xt_free = [[] for _ in range(3)]
                    lnb_free = [[] for _ in range(2)]
                    st_war = [{"war": []} for _ in range(3)]
                    lnxT_evac = []

                    def ln_finish(tt):
                        bi = tt % 3
                        li = tt % 2
                        lnb = lnb_bufs[li]
                        st = st_war[bi]
                        s4 = _ln_b(sc, xt[bi][:, :], st, lnb[:, :], out_deps=lnb_free[li])
                        xt_free[bi] = [("v", s4)]
                        evs = []
                        for g2 in range(2):
                            j, pbx, fdep = bbank.acquire()
                            tdeps = [("v", s4), ("yc", ld_const_y)] + ([fdep] if fdep else [])
                            tl = None
                            for k2 in range(8):
                                k = g2 * 8 + k2
                                tl = sc.op("t", lambda e, o=pbx[:, k2, :],
                                           i=lnb[:, k * 128:(k + 1) * 128], idq=idm_b[:, :]:
                                           e.transpose(o, i, idq),
                                           deps=tdeps if k2 == 0 else [("v", s4)],
                                           inc=(k2 == 7))
                            ev = sc.op("s", lambda e, o=lnxT[:, g2 * 8:(g2 + 1) * 8, tt * 128:(tt + 1) * 128],
                                       i=pbx[:, :, :]: e.copy(o, i), deps=[("t", tl)])
                            bbank.consumed(j, "s", ev)
                            evs.append(ev)
                        lnb_free[li] = [("t", sc.cnt["t"])]
                        lnxT_evac.append(evs[-1])

                    for tt in range(9):
                        bi = tt % 3
                        r0 = base + tt * 128
                        ldx = sc.dma("y", "xt%d" % bi, lambda e, o=xt[bi][:, :], i=xh[r0:r0 + 128, :]:
                                     e.dma_start(o, i), deps=xt_free[bi])
                        st = st_war[bi]
                        for nm in stats:
                            st[nm] = stats[nm][bi][:, :]
                        st["sq"] = sqs[:, :]
                        _ln_a(sc, xt[bi][:, :], st, deps=[("xt%d" % bi, ldx)])
                        if tt >= 1:
                            ln_finish(tt - 1)
                    ln_finish(8)
                    lnxT_all = [("s", max(lnxT_evac))]
                sc.barrier()

                with _ES() as bsc:
                    bb = bsc.enter_context
                    QT = bb(nc.sbuf_tensor("QT%d" % sp, [128, 1024], BF16))
                    KT = bb(nc.sbuf_tensor("KT%d" % sp, [128, HALO + 1024], BF16))
                    qb_t = [bb(nc.sbuf_tensor("qb_t%d_%d" % (sp, i), [128, 512], BF16)) for i in range(5)]
                    qs_t = [bb(nc.sbuf_tensor("qs_t%d_%d" % (sp, i), [128, 512], BF16)) for i in range(5)]
                    qc_t = [bb(nc.sbuf_tensor("qc_t%d_%d" % (sp, i), [128, 512], BF16)) for i in range(5)]
                    rope_free = [[] for _ in range(5)]
                    v4 = [bb(nc.sbuf_tensor("v4_%d_%d" % (sp, i), [128, 9, 512], BF16)) for i in range(2)]
                    wq_b = [bb(nc.sbuf_tensor("wq_b%d_%d" % (sp, i), [128, NK, 128], BF16)) for i in range(2)]
                    wk_b = [bb(nc.sbuf_tensor("wk_b%d_%d" % (sp, i), [128, NK, 128], BF16)) for i in range(2)]
                    wv_b = [bb(nc.sbuf_tensor("wv_b%d_0" % sp, [128, NK, 512], BF16))]
                    Pf = [bb(nc.sbuf_tensor("Pf%d_%d" % (sp, i), [128, 256], F32)) for i in range(8)]
                    Pbf = [[bb(nc.sbuf_tensor("Pbf%d_%d_%d" % (sp, pp, i), [128, 256], BF16))
                            for i in range(8)] for pp in range(2)]
                    PTb = [bb(nc.sbuf_tensor("PT%d_%d" % (sp, i), [128, 2, 128], BF16)) for i in range(8)]
                    smx = bb(nc.sbuf_tensor("smx%d" % sp, [128, 32], F32))
                    wq_free = [[], []]
                    wk_free = [[], []]
                    wv_free = [[]]
                    Pf_free = [[] for _ in range(8)]
                    Pbf_free = [[[] for _ in range(8)] for _ in range(2)]
                    PT_free = [[] for _ in range(8)]
                    v4_ready = [[], []]
                    ptav_free = [([bbank.free_dep[i]] if bbank.free_dep[i] else [])
                                 for i in range(2)]

                    def emit_ptav(ph, pvnrs, pgvi):
                        ppar = ph % 2
                        spts = [None] * 8
                        glast = [None, None]
                        for g in range(2):
                            pbx = pb[g]
                            tp = None
                            for qi in range(4):
                                qt = g * 4 + qi
                                sl = qi * 2
                                for half in range(2):
                                    tp = sc.op("t", lambda e, o=pbx[:, sl + half, :],
                                               i=Pbf[ppar][qt][:, half * 128:(half + 1) * 128], idq=idm_b[:, :]:
                                               e.transpose(o, i, idq),
                                               deps=[("v", pvnrs[qt])] + (ptav_free[g] if qi == 0 and half == 0 else []),
                                               inc=(qi == 3 and half == 1))
                                Pbf_free[ppar][qt] = [("t", tp)]
                            glast[g] = tp
                        for g in range(2):
                            pbx = pb[g]
                            last_spt = None
                            for qi in range(4):
                                qt = g * 4 + qi
                                sl = qi * 2
                                last_spt = sc.op("s", lambda e, o=PTb[qt][:, :, :], i=pbx[:, sl:sl + 2, :]:
                                                 e.copy(o, i), deps=[("t", glast[g])] + PT_free[qt])
                                spts[qt] = last_spt
                            ptav_free[g] = [("s", last_spt)]
                        for qt in range(8):
                            j2, psav, fdep2 = fbank.acquire()
                            hx = ph % 4
                            sc.op("t", lambda e, o=psav[:, :128], l=v4[pgvi][:, qt, hx * 128:(hx + 1) * 128],
                                  r=PTb[qt][:, 0, :]: e.matmul(o, l, r, start=True, stop=False),
                                  deps=v4_ready[pgvi] + [("s", spts[qt])] + ([fdep2] if fdep2 else []), inc=False)
                            ta = sc.op("t", lambda e, o=psav[:, :128], l=v4[pgvi][:, qt + 1, hx * 128:(hx + 1) * 128],
                                       r=PTb[qt][:, 1, :]: e.matmul(o, l, r, start=False, stop=True),
                                       deps=[("s", spts[qt])])
                            PT_free[qt] = [("t", ta)]
                            sae = sc.op("s", lambda e, o=attnT[:, ph, qt * 128:(qt + 1) * 128],
                                        i=psav[:, :128]: e.activation(o, i, AF.Copy), deps=[("t", ta)])
                            fbank.consumed(j2, "s", sae)

                    pend = None
                    for h in range(NH):
                        emit_conv(1)
                        wbi = h % 2
                        ldq = sc.dma("g", "wq%d" % wbi, lambda e, o=wq_b[wbi][:, :, :],
                                     i=wq[:, h * 128:(h + 1) * 128].rearrange("(k p) d -> p k d", p=128):
                                     e.dma_start(o, i), deps=wq_free[wbi])
                        ldk = sc.dma("g", "wk%d" % wbi, lambda e, o=wk_b[wbi][:, :, :],
                                     i=wk[:, h * 128:(h + 1) * 128].rearrange("(k p) d -> p k d", p=128):
                                     e.dma_start(o, i), deps=wk_free[wbi])
                        gvi = 0
                        vpar = (h // 4) % 2
                        if h % 4 == 0:
                            ldv = sc.dma(
                                "g", "wv%d" % gvi, lambda e, o=wv_b[gvi][:, :, :],
                                i=wv[:, h * 128:(h + 4) * 128].rearrange("(k p) d -> p k d", p=128):
                                e.dma_start(o, i), deps=wv_free[gvi])
                            v_ev = []
                            for t9 in range(9):
                                j, psx, fdep = fbank.acquire()
                                mmdeps = lnxT_all + [("wv%d" % gvi, ldv)] + ([fdep] if fdep else [])
                                tl = None
                                for k in range(NK):
                                    tl = sc.op("t", lambda e, o=psx[:, :], l=lnxT[:, k, t9 * 128:(t9 + 1) * 128],
                                               r=wv_b[gvi][:, k, :], st=(k == 0), sp_=(k == NK - 1):
                                               e.matmul(o, l, r, start=st, stop=sp_),
                                               deps=mmdeps, inc=(k == NK - 1))
                                ev = sc.op("s", lambda e, o=v4[vpar][:, t9, :], i=psx[:, :]:
                                           e.activation(o, i, AF.Copy), deps=[("t", tl)])
                                fbank.consumed(j, "s", ev)
                                v_ev.append(ev)
                            v4_ready[vpar] = [("s", max(v_ev))]
                            wv_free[gvi] = [("t", sc.cnt["t"])]

                        chunks = ([(QT, 128, c0, cw, wq_b[wbi], "wq%d" % wbi, ldq)
                                   for (c0, cw) in [(128, 512), (640, 512)]] +
                                  [(KT, 0, c0, cw, wk_b[wbi], "wk%d" % wbi, ldk)
                                   for (c0, cw) in [(0, 512), (512, 512), (1024, 128)]])
                        ses = []
                        for ci, (dst, off0, c0, cw, wbuf, wslot, ldw) in enumerate(chunks):
                            j, psx, fdep = fbank.acquire()
                            mmdeps = lnxT_all + [(wslot, ldw)] + ([fdep] if fdep else [])
                            tl = None
                            for k in range(NK):
                                tl = sc.op("t", lambda e, o=psx[:, :cw], l=wbuf[:, k, :],
                                           r=lnxT[:, k, c0:c0 + cw], st=(k == 0), sp_=(k == NK - 1):
                                           e.matmul(o, l, r, start=st, stop=sp_),
                                           deps=mmdeps, inc=(k == NK - 1))
                            se = sc.op("s", lambda e, o=qb_t[ci][:, :cw], i=psx[:, :cw]:
                                       e.activation(o, i, AF.Copy),
                                       deps=[("t", tl)] + rope_free[ci])
                            fbank.consumed(j, "s", se)
                            ses.append(se)
                        vqs = []
                        for ci, (dst, off0, c0, cw, wbuf, wslot, ldw) in enumerate(chunks):
                            gc0 = base + c0
                            vqs.append(sc.op("v", lambda e, o=qs_t[ci][:, :cw], i0=qb_t[ci][:, :cw],
                                             i1=sin_b[:, gc0:gc0 + cw]:
                                             e.tensor_mul(o, i0, i1),
                                             deps=[("s", ses[ci]), ("gc", ld_const_g)]))
                        vcs = []
                        for ci, (dst, off0, c0, cw, wbuf, wslot, ldw) in enumerate(chunks):
                            gc0 = base + c0
                            vcs.append(sc.op("v", lambda e, o=qc_t[ci][:, :cw], i0=qb_t[ci][:, :cw],
                                             i1=cos_b[:, gc0:gc0 + cw]:
                                             e.tensor_mul(o, i0, i1), deps=[("s", ses[ci])]))
                        trs = []
                        for ci, (dst, off0, c0, cw, wbuf, wslot, ldw) in enumerate(chunks):
                            j2, psr, fdep2 = fbank.acquire()
                            trs.append((j2, psr, sc.op(
                                "t", lambda e, o=psr[:, :cw], l=rt_b[:, :], r=qs_t[ci][:, :cw]:
                                e.matmul(o, l, r, start=True, stop=True),
                                deps=[("v", vqs[ci]), ("yc", ld_const_y)] + ([fdep2] if fdep2 else []))))
                        vfs = []
                        for ci, (dst, off0, c0, cw, wbuf, wslot, ldw) in enumerate(chunks):
                            j2, psr, tr = trs[ci]
                            vf = sc.op("v", lambda e, o=dst[:, c0 - off0:c0 - off0 + cw],
                                       i0=qc_t[ci][:, :cw], i1=psr[:, :cw]:
                                       e.tensor_add(o, i0, i1), deps=[("v", vcs[ci]), ("t", tr)])
                            fbank.consumed(j2, "v", vf)
                            rope_free[ci] = [("v", vf), ("t", tr)]
                            vfs.append(vf)
                        wq_free[wbi] = [("t", sc.cnt["t"])]
                        wk_free[wbi] = [("t", sc.cnt["t"])]
                        qk_ready = [("v", max(vfs))]

                        vms, vmxs, sxps, vsms, vrcs = [], [], [], [], []
                        for qt in range(8):
                            j, psx, fdep = fbank.acquire()
                            ts_ = sc.op("t", lambda e, o=psx[:, :256], l=QT[:, qt * 128:(qt + 1) * 128],
                                        r=KT[:, qt * 128:qt * 128 + 256]:
                                        e.matmul(o, l, r, start=True, stop=True),
                                        deps=qk_ready + ([fdep] if fdep else []))
                            mask = mA_b if (sp == 0 and qt == 0) else mB_b
                            vm = sc.op("v", lambda e, o=Pf[qt][:, :], i=psx[:, :256], m=mask[:, :]:
                                       e.scalar_tensor_tensor(o, i, SCALE, m, ALU.mult, ALU.add),
                                       deps=[("t", ts_), ("yc", ld_const_y)] + Pf_free[qt])
                            fbank.consumed(j, "v", vm)
                            vms.append(vm)
                        for qt in range(8):
                            vmxs.append(sc.op("v", lambda e, o=smx[:, qt * 4:qt * 4 + 1], i=Pf[qt][:, :]:
                                              e.reduce_max(o, i, axis=AX.X, negate=True),
                                              deps=[("v", vms[qt])]))
                        for qt in range(8):
                            sxps.append(sc.op("s", lambda e, o=Pf[qt][:, :], i=Pf[qt][:, :],
                                              b=smx[:, qt * 4:qt * 4 + 1]:
                                              e.activation(o, i, AF.Exp, bias=b),
                                              deps=[("v", vmxs[qt])]))
                        for qt in range(8):
                            vsms.append(sc.op("v", lambda e, o=smx[:, qt * 4 + 1:qt * 4 + 2], i=Pf[qt][:, :]:
                                              e.reduce_sum(o, i, axis=AX.X), deps=[("s", sxps[qt])]))
                        for qt in range(8):
                            vrcs.append(sc.op("v", lambda e, o=smx[:, qt * 4 + 2:qt * 4 + 3],
                                              i=smx[:, qt * 4 + 1:qt * 4 + 2]:
                                              e.reciprocal(o, i), deps=[("v", vsms[qt])]))
                        vnrs = []
                        for qt in range(8):
                            vnr = sc.op("v", lambda e, o=Pbf[h % 2][qt][:, :], i=Pf[qt][:, :],
                                        r=smx[:, qt * 4 + 2:qt * 4 + 3]:
                                        e.tensor_scalar(o, i, r, None, ALU.mult),
                                        deps=[("v", vrcs[qt])] + Pbf_free[h % 2][qt])
                            Pf_free[qt] = [("v", vnr)]
                            vnrs.append(vnr)
                        if pend is not None:
                            emit_ptav(*pend)
                        pend = (h, vnrs, vpar)

                    emit_ptav(*pend)
                    bbank.consumed(0, "s", sc.cnt["s"])
                    bbank.consumed(1, "s", sc.cnt["s"])
                    attnT_all = [("s", sc.cnt["s"])]
                sc.barrier()

                with _ES() as b3sc:
                    b3 = b3sc.enter_context
                    wo_b = [b3(nc.sbuf_tensor("wo_b%d_%d" % (sp, i), [128, NK, 512], BF16)) for i in range(2)]
                    xc_b = [b3(nc.sbuf_tensor("xc_b%d_%d" % (sp, i), [128, 512], F32)) for i in range(2)]
                    x1c = [b3(nc.sbuf_tensor("x1c%d_%d" % (sp, i), [128, 512], F32)) for i in range(2)]
                    x1t = [b3(nc.sbuf_tensor("x1t%d_%d" % (sp, i), [128, D], F32)) for i in range(3)]
                    sqsB = b3(nc.sbuf_tensor("sqB%d" % sp, [128, D], F32))
                    ln2b = [b3(nc.sbuf_tensor("ln2b%d_%d" % (sp, i), [128, D], BF16))
                            for i in range(2)]
                    l2tb = [b3(nc.sbuf_tensor("l2tb%d_%d" % (sp, i), [128, NK, 128], BF16)) for i in range(2)]
                    statsB = {}
                    for nm in ("ssum", "ssq", "mean", "m2", "var", "rstd"):
                        statsB[nm] = [b3(nc.sbuf_tensor("stB%d_%s%d" % (sp, nm, i), [128, 1], F32))
                                      for i in range(3)]
                    gsc = {nm: b3(nc.sbuf_tensor("g%d_%s" % (sp, nm),
                                                 [128, NE if nm in ("lg", "im", "mk", "e1") else 1], F32))
                           for nm in ("lg", "im", "mk", "e1", "nm1", "nm2", "den", "rcd")}
                    gbf = b3(nc.sbuf_tensor("gbf%d" % sp, [128, NE], BF16))
                    wo_free = [[], []]
                    xc_free = [[], []]
                    x1c_free = [[], []]
                    x1t_free = [[], [], []]
                    l2tb_free = [[], []]
                    ln2b_free = [[], []]
                    stB_war = [{"war": []} for _ in range(3)]

                    for dc in range(4):
                        emit_conv(1)
                        wbi = dc % 2
                        ldw = sc.dma("g", "wo%d" % wbi, lambda e, o=wo_b[wbi][:, :, :],
                                     i=wo[:, dc * 512:(dc + 1) * 512].rearrange("(k p) d -> p k d", p=128):
                                     e.dma_start(o, i), deps=wo_free[wbi])
                        for tt in range(8):
                            bi = tt % 2
                            j, psx, fdep = fbank.acquire()
                            mmdeps = attnT_all + [("wo%d" % wbi, ldw), ("gc", ld_const_g),
                                                  ("yc", ld_const_y)] + ([fdep] if fdep else [])
                            for k in range(NK):
                                sc.op("t", lambda e, o=psx[:, :], l=attnT[:, k, tt * 128:(tt + 1) * 128],
                                      r=wo_b[wbi][:, k, :], st=(k == 0):
                                      e.matmul(o, l, r, start=st, stop=False),
                                      deps=mmdeps, inc=False)
                            tl = sc.op("t", lambda e, o=psx[:, :], l=ones_b[:, :],
                                       r=bo_b[:, dc * 512:(dc + 1) * 512]:
                                       e.matmul(o, l, r, start=False, stop=True), deps=mmdeps)
                            r0 = HALO + sp * 1024 + tt * 128
                            ldxc = sc.dma("y", "xc%d" % bi, lambda e, o=xc_b[bi][:, :], i=xh[r0:r0 + 128, dc * 512:(dc + 1) * 512]:
                                          e.dma_start(o, i), deps=xc_free[bi])
                            vx = sc.op("v", lambda e, o=x1c[bi][:, :], i0=psx[:, :], i1=xc_b[bi][:, :]:
                                       e.tensor_add(o, i0, i1),
                                       deps=[("t", tl), ("xc%d" % bi, ldxc)] + x1c_free[bi])
                            fbank.consumed(j, "v", vx)
                            xc_free[bi] = [("v", vx)]
                            rl = sp * 1024 + tt * 128
                            st1 = sc.dma("s", "sx%d" % bi, lambda e, o=x1s[rl:rl + 128, dc * 512:(dc + 1) * 512],
                                         i=x1c[bi][:, :]: e.dma_start(o, i), deps=[("v", vx)])
                            x1c_free[bi] = [("sx%d" % bi, st1)]
                            x1_store[(sp, tt, dc)] = st1
                        wo_free[wbi] = [("t", sc.cnt["t"])]

                    def p2_finish(tt):
                        bi = tt % 3
                        li = tt % 2
                        rl = sp * 1024 + tt * 128
                        gt_g = sp * 8 + tt
                        st = stB_war[bi]
                        s4 = _ln_b(sc, x1t[bi][:, :], st, ln2b[li][:, :],
                                   out_deps=ln2b_free[li])
                        x1t_free[bi] = [("v", s4)]
                        str_ = sc.dma("s", "sr%d" % li, lambda e, o=l2r[rl:rl + 128, :],
                                      i=ln2b[li][:, :]: e.dma_start(o, i), deps=[("v", s4)])
                        evs = []
                        for g2 in range(2):
                            j, pbx, fdep = bbank.acquire()
                            tdeps = [("v", s4)] + ([fdep] if fdep else [])
                            tl = None
                            for k2 in range(8):
                                k = g2 * 8 + k2
                                tl = sc.op("t", lambda e, o=pbx[:, k2, :],
                                           i=ln2b[li][:, k * 128:(k + 1) * 128], idq=idm_b[:, :]:
                                           e.transpose(o, i, idq),
                                           deps=tdeps if k2 == 0 else [("v", s4)],
                                           inc=(k2 == 7))
                            ev = sc.op("s", lambda e, o=l2tb[li][:, g2 * 8:(g2 + 1) * 8, :], i=pbx[:, :, :]:
                                       e.copy(o, i), deps=[("t", tl)] + (l2tb_free[li] if g2 == 0 else []))
                            bbank.consumed(j, "s", ev)
                            evs.append(ev)
                        ln2b_free[li] = [("t", sc.cnt["t"]), ("sr%d" % li, str_)]
                        stl = sc.dma("s", "sl%d" % li, lambda e, o=l2s.rearrange("(k p) t -> p k t", p=128)[:, :, rl:rl + 128],
                                     i=l2tb[li][:, :, :]: e.dma_start(o, i), deps=[("s", max(evs))])
                        j, psx, fdep = fbank.acquire()
                        tl = None
                        for k in range(NK):
                            tl = sc.op("t", lambda e, o=psx[:, :NE], l=l2tb[li][:, k, :],
                                       r=gw_b[:, k, :], st=(k == 0), sp_=(k == NK - 1):
                                       e.matmul(o, l, r, start=st, stop=sp_),
                                       deps=[("s", max(evs)), ("gc", ld_const_g)] + ([fdep] if fdep else []),
                                       inc=(k == NK - 1))
                        l2tb_free[li] = [("t", tl), ("sl%d" % li, stl)]
                        lg = gsc["lg"][:, :]
                        v0 = sc.op("v", lambda e, o=lg, i=psx[:, :NE]: e.tensor_copy(o, i),
                                   deps=[("t", tl), ("v", sc.cnt["v"]), ("s", sc.cnt["s"])])
                        fbank.consumed(j, "v", v0)
                        v1 = sc.op("v", lambda e, o=gsc["nm1"][:, :], i=lg:
                                   e.reduce_max(o, i, axis=AX.X, negate=True), deps=[("v", v0)])
                        v2 = sc.op("v", lambda e, o=gsc["im"][:, :], i=lg, m=gsc["nm1"][:, :]:
                                   e.tensor_scalar(o, i, m, 0.0, ALU.add, ALU.is_ge), deps=[("v", v1)])
                        v3 = sc.op("v", lambda e, o=gsc["mk"][:, :], i0=gsc["im"][:, :], i1=lg:
                                   e.scalar_tensor_tensor(o, i0, NEG, i1, ALU.mult, ALU.add),
                                   deps=[("v", v2)])
                        v4_ = sc.op("v", lambda e, o=gsc["nm2"][:, :], i=gsc["mk"][:, :]:
                                    e.reduce_max(o, i, axis=AX.X, negate=True), deps=[("v", v3)])
                        s1 = sc.op("s", lambda e, o=gsc["e1"][:, :], i=lg, b=gsc["nm1"][:, :]:
                                   e.activation(o, i, AF.Exp, bias=b), deps=[("v", v1), ("v", v3)])
                        s2 = sc.op("s", lambda e, o=gsc["den"][:, :], i=gsc["nm2"][:, :], b=gsc["nm1"][:, :]:
                                   e.activation(o, i, AF.Exp, bias=b, scale=-1.0), deps=[("v", v4_), ("s", s1)])
                        v5 = sc.op("v", lambda e, o=gsc["den"][:, :], i=gsc["den"][:, :]:
                                   e.tensor_scalar(o, i, 1.0, None, ALU.add), deps=[("s", s2)])
                        v6 = sc.op("v", lambda e, o=gsc["rcd"][:, :], i=gsc["den"][:, :]:
                                   e.reciprocal(o, i), deps=[("v", v5)])
                        v7 = sc.op("v", lambda e, o=gsc["mk"][:, :], i=lg, m=gsc["nm2"][:, :]:
                                   e.tensor_scalar(o, i, m, 0.0, ALU.add, ALU.is_ge), deps=[("v", v4_), ("s", s2)])
                        v8 = sc.op("v", lambda e, o=gsc["e1"][:, :], i=gsc["e1"][:, :], m=gsc["rcd"][:, :]:
                                   e.tensor_scalar(o, i, m, None, ALU.mult), deps=[("v", v6), ("s", s1)])
                        v9 = sc.op("v", lambda e, o=gates[:, gt_g * NE:(gt_g + 1) * NE],
                                   i0=gsc["e1"][:, :], i1=gsc["mk"][:, :]:
                                   e.tensor_mul(o, i0, i1), deps=[("v", v7), ("v", v8)])
                        sg = sc.op("s", lambda e, o=gbf[:, :], i=gates[:, gt_g * NE:(gt_g + 1) * NE]:
                                   e.copy(o, i), deps=[("v", v9)])
                        jb, pbx, fdepb = bbank.acquire()
                        tg = sc.op("t", lambda e, o=pbx[:NE, 0, :], i=gbf[:, :], idq=idm_b[:, :]:
                                   e.transpose(o, i, idq), deps=[("s", sg)] + ([fdepb] if fdepb else []))
                        sgt = sc.op("s", lambda e, o=gatesT[:, gt_g * 128:(gt_g + 1) * 128], i=pbx[:NE, 0, :]:
                                    e.copy(o, i), deps=[("t", tg)])
                        bbank.consumed(jb, "s", sgt)

                    for tt in range(8):
                        bi = tt % 3
                        rl = sp * 1024 + tt * 128
                        ldx1 = sc.dma("y", "x1t%d" % bi, lambda e, o=x1t[bi][:, :], i=x1s[rl:rl + 128, :]:
                                      e.dma_start(o, i),
                                      deps=[("sx0", sc.dslot.get("sx0", 0)),
                                            ("sx1", sc.dslot.get("sx1", 0))] + x1t_free[bi])
                        st = stB_war[bi]
                        for nm in statsB:
                            st[nm] = statsB[nm][bi][:, :]
                        st["sq"] = sqsB[:, :]
                        _ln_a(sc, x1t[bi][:, :], st, deps=[("x1t%d" % bi, ldx1)])
                        if tt >= 1:
                            p2_finish(tt - 1)
                    p2_finish(7)
                sc.barrier()

        if debug:
            sc.dma("s", "so", lambda e: e.dma_start(dbg_g[:, :], gates[:, :]))
        sc.flush()


        from concourse.bass_core_rust import IndirectOffsetOnAxis

        abw.close()

        with ExitStack() as c0:
            a = c0.enter_context
            selT = a(nc.sbuf_tensor("selT", [NE, TOK], F32))
            pfx = [a(nc.sbuf_tensor("pfx%d" % i, [NE, TOK], F32)) for i in range(2)]
            mp_tok = a(nc.sbuf_tensor("mp_tok", [128, 16, NE], F32))
            Bt = a(nc.sbuf_tensor("Bt", [128, 16, CAP], F32))
            comb = a(nc.sbuf_tensor("comb", [128, 16, 3], F32))
            idxg = a(nc.sbuf_tensor("idxg", [4, CAP], F32))
            iop1_b = a(nc.sbuf_tensor("iop1_b", [128, CAP], F32))
            tokv_b = a(nc.sbuf_tensor("tokv_b", [128, 16], F32))

            BIGC = 65536.0
            sc.dma("y", "yc2", lambda e: e.dma_start(iop1_b[:, :], iotap1))
            ldc2 = sc.dma("y", "yc2", lambda e: e.dma_start(tokv_b[:, :], tokv))
            v0 = sc.op("v", lambda e: e.tensor_scalar(selT[:, :], gatesT[:, :], 0.0,
                                                      None, ALU.is_gt),
                       deps=[("s", sc.cnt["s"])])
            vp = sc.op("v", lambda e: e.tensor_copy(pfx[0][:, :], selT[:, :]),
                       deps=[("v", v0)])
            cur = 0
            sh = 1
            while sh < TOK:
                nxt = 1 - cur
                vp1 = sc.op("v", lambda e, c=cur, n=nxt, s_=sh:
                            e.tensor_add(pfx[n][:, s_:], pfx[c][:, s_:], pfx[c][:, :TOK - s_]),
                            deps=[("v", vp)])
                vp = sc.op("v", lambda e, c=cur, n=nxt, s_=sh:
                           e.tensor_copy(pfx[n][:, :s_], pfx[c][:, :s_]),
                           deps=[("v", vp1)])
                cur = nxt
                sh *= 2
            vb = sc.op("v", lambda e, c=cur: e.tensor_scalar(pfx[1 - c][:, :], pfx[c][:, :],
                                                             BIGC, None, ALU.add),
                       deps=[("v", vp)])
            vm_ = sc.op("v", lambda e, c=cur: e.scalar_tensor_tensor(
                pfx[c][:, :], selT[:, :], -BIGC, pfx[1 - c][:, :], ALU.mult, ALU.add),
                deps=[("v", vb)])
            j, psx, fdep = fbank.acquire()
            tl = None
            for g in range(16):
                tl = sc.op("t", lambda e, o=psx[:, g * 8:(g + 1) * 8],
                           i=pfx[cur][:, g * 128:(g + 1) * 128], idq=idmf_b[:NE, :NE]:
                           e.transpose(o, i, idq),
                           deps=[("v", vm_)] + ([fdep] if fdep and g == 0 else []),
                           inc=(g == 15))
            sev = sc.op("s", lambda e, i=psx[:, :128].rearrange("p (g e) -> p g e", e=NE):
                        e.copy(mp_tok[:, :, :], i), deps=[("t", tl)])
            fbank.consumed(j, "s", sev)
            ite = []
            for ei in range(NE):
                bts = []
                for g in range(16):
                    bts.append(sc.op("v", lambda e, g=g, ei=ei:
                                     e.tensor_scalar(Bt[:, g, :], iop1_b[:, :],
                                                     mp_tok[:, g, ei:ei + 1], None, ALU.is_equal),
                                     deps=[("s", sev), ("yc2", ldc2),
                                           ("t", sc.cnt["t"]), ("s", sc.cnt["s"])]))
                vcb = sc.op("v", lambda e, ei=ei: e.tensor_copy(comb[:, :, 1:2],
                                                                gates[:, ei::NE].rearrange("p (g o) -> p g o", o=1)),
                            deps=[("v", sc.cnt["v"]), ("t", sc.cnt["t"])])
                vcb2 = sc.op("v", lambda e: e.tensor_copy(comb[:, :, 0:1],
                                                          tokv_b[:, :].rearrange("p (g o) -> p g o", o=1)),
                             deps=[("yc2", ldc2), ("t", sc.cnt["t"])])
                vcb3 = sc.op("v", lambda e: e.memset(comb[:, :, 2:3], 1.0),
                             deps=[("t", sc.cnt["t"])])
                evs2 = []
                for (cs0, csw) in ((0, 512), (512, 128)):
                    j, psx, fdep = fbank.acquire()
                    tl = None
                    for g in range(16):
                        tl = sc.op("t", lambda e, g=g, cs0=cs0, csw=csw,
                                   o=psx[:3, :csw]:
                                   e.matmul(o, comb[:, g, :], Bt[:, g, cs0:cs0 + csw],
                                            start=(g == 0), stop=(g == 15)),
                                   deps=[("v", max(bts)), ("v", vcb), ("v", vcb2), ("v", vcb3)] +
                                        ([fdep] if fdep and g == 0 else []),
                                   inc=(g == 15))
                    ev2 = sc.op("s", lambda e, o=idxg[:3, cs0:cs0 + csw], i=psx[:3, :csw]:
                                e.copy(o, i), deps=[("t", tl)])
                    fbank.consumed(j, "s", ev2)
                    evs2.append(ev2)
                sev2 = max(evs2)
                j2, ps2, fdep2 = fbank.acquire()
                tl2 = None
                for t5 in range(5):
                    tl2 = sc.op("t", lambda e, t5=t5, o=ps2[:, t5 * 4:t5 * 4 + 3]:
                                e.transpose(o, idxg[:3, t5 * 128:(t5 + 1) * 128], idmf_b[:3, :3]),
                                deps=[("s", sev2)] + ([fdep2] if fdep2 and t5 == 0 else []),
                                inc=(t5 == 4))
                sev3 = sc.op("s", lambda e, o=idxT[:, ei * 5:(ei + 1) * 5, :],
                             i=ps2[:, :20].rearrange("p (t c) -> p t c", c=4):
                             e.copy(o, i), deps=[("t", tl2)])
                fbank.consumed(j2, "s", sev3)
                ite.append(sev3)
            vii = sc.op("v", lambda e: e.tensor_copy(idx_i[:, :], idxT[:, :, 0]),
                        deps=[("s", max(ite))])
            idx_ready = [("v", vii), ("s", max(ite))]

        sc.barrier()

        gsb_free = [[], []]
        lrg_free = [[], []]
        bounce_done = {}

        def emit_bounce(ei):
            par = ei % 2
            stores = []
            for t5 in range(5):
                gb = t5 % 2
                gx = sc.dma("g", "gx%d" % gb, lambda e, t5=t5, ei=ei, o=g_sb[gb][:, :]:
                            e.indirect_dma_start(
                                o, None, l2r,
                                IndirectOffsetOnAxis(idx_i[:, ei * 5 + t5:ei * 5 + t5 + 1], 0)),
                            deps=idx_ready + sr_full + gsb_free[gb])
                gr = sc.dma("s", "gr%d%d" % (par, gb), lambda e, t5=t5, par=par, gb=gb,
                            o=l2rg[par, t5 * 128:(t5 + 1) * 128, :], i=g_sb[gb][:, :]:
                            e.dma_start(o, i), deps=[("gx%d" % gb, gx)] + lrg_free[par])
                gsb_free[gb] = [("gr%d%d" % (par, gb), gr)]
                stores.append(("gr%d%d" % (par, gb), gr))
            bounce_done[ei] = stores[-2:]

        l2t_free = []
        with ExitStack() as cs:
            a = cs.enter_context
            l2t = a(nc.sbuf_tensor("l2t", [128, NK, 1024], BF16))
            w1b = [a(nc.sbuf_tensor("w1b%d" % i, [128, NK, 512], BF16)) for i in range(2)]
            w2b = [a(nc.sbuf_tensor("w2b%d" % i, [128, 8, 512], BF16)) for i in range(2)]
            hraw = a(nc.sbuf_tensor("hraw", [128, 8, 1024], BF16))
            acc = a(nc.sbuf_tensor("acc", [128, 8, 2048], F32))
            x1cb = [a(nc.sbuf_tensor("x1cb%d" % i, [128, 512], F32)) for i in range(2)]

            w1_free = [[], []]
            w2_free = [[], []]
            x1cb_free = [[], []]
            w1_i = 0
            w2_i = 0
            sr_full = [("sr0", sc.dslot.get("sr0", 0)), ("sr1", sc.dslot.get("sr1", 0))]
            vms0 = sc.op("v", lambda e: e.memset(g_sb[0][:, :], 0.0), deps=[])
            vms1 = sc.op("v", lambda e: e.memset(g_sb[1][:, :], 0.0), deps=[])
            gsb_free[0] = [("v", vms0)]
            gsb_free[1] = [("v", vms1)]
            emit_bounce(0)
            emit_bounce(1)

            for s in range(2):
                tok0 = s * 1024
                ldl2 = sc.dma("y", "l2t", lambda e, o=l2t[:, :, :],
                              i=l2s.rearrange("(k p) t -> p k t", p=128)[:, :, tok0:tok0 + 1024]:
                              e.dma_start(o, i),
                              deps=[("sl0", sc.dslot.get("sl0", 0)),
                                    ("sl1", sc.dslot.get("sl1", 0))] + l2t_free)
                acc_v = [[0] * 4 for _ in range(8)]
                so_before = sc.dslot.get("so", 0)
                for ei in (8, 9):
                    half = ei - NE
                    w1blocks = [sw1[:, half * 1024:half * 1024 + 512],
                                sw1[:, half * 1024 + 512:half * 1024 + 1024]]
                    w2q = lambda dc, half=half: sw2[half * 1024:(half + 1) * 1024,
                                                    dc * 512:(dc + 1) * 512]
                    bcol = lambda hc, half=half: sb1t_b[:, half * 8 + hc:half * 8 + hc + 1]
                    hdone = []
                    for blk in range(2):
                        wi = w1_i % 2
                        w1_i += 1
                        ldw1 = sc.dma("g", "w1%d" % wi, lambda e, o=w1b[wi][:, :, :],
                                      i=w1blocks[blk].rearrange("(k p) h -> p k h", p=128):
                                      e.dma_start(o, i), deps=w1_free[wi])
                        for hc2 in range(4):
                            hcf = blk * 4 + hc2
                            for tc in range(2):
                                j, psx, fdep = fbank.acquire()
                                mmdeps = [("w1%d" % wi, ldw1), ("l2t", ldl2)] + \
                                         ([fdep] if fdep else [])
                                tl = None
                                for k in range(NK):
                                    tl = sc.op("t", lambda e, o=psx[:, :], l=w1b[wi][:, k, hc2 * 128:(hc2 + 1) * 128],
                                               r=l2t[:, k, tc * 512:(tc + 1) * 512],
                                               st=(k == 0), sp_=(k == NK - 1):
                                               e.matmul(o, l, r, start=st, stop=sp_),
                                               deps=mmdeps, inc=(k == NK - 1))
                                sh = sc.op("s", lambda e, o=hraw[:, hcf, tc * 512:(tc + 1) * 512],
                                           i=psx[:, :], b=bcol(hcf):
                                           e.activation(o, i, AF.Gelu, bias=b), deps=[("t", tl)])
                                fbank.consumed(j, "s", sh)
                                hdone.append(sh)
                        w1_free[wi] = [("t", sc.cnt["t"])]
                    h_ready = [("s", max(hdone))]
                    for dc in range(4):
                        wi = w2_i % 2
                        w2_i += 1
                        ldw2 = sc.dma("g", "w2%d" % wi, lambda e, o=w2b[wi][:, :, :],
                                      i=w2q(dc).rearrange("(k p) d -> p k d", p=128):
                                      e.dma_start(o, i), deps=w2_free[wi])
                        for tt in range(8):
                            j, psx, fdep = fbank.acquire()
                            mmdeps = h_ready + [("w2%d" % wi, ldw2), ("gc", ld_const_g),
                                                ("yc", ld_const_y)] + ([fdep] if fdep else [])
                            tl = None
                            for k in range(8):
                                last = (k == 7) and ei != 9
                                tl = sc.op("t", lambda e, o=psx[:, :], l=hraw[:, k, tt * 128:(tt + 1) * 128],
                                           r=w2b[wi][:, k, :], st=(k == 0), sp_=last:
                                           e.matmul(o, l, r, start=st, stop=sp_),
                                           deps=mmdeps, inc=last)
                            if ei == 9:
                                gt_g = s * 8 + tt
                                sc.op("t", lambda e, o=psx[:, :], l=gatesT[:, gt_g * 128:(gt_g + 1) * 128],
                                      r=b2_b[:, dc * 512:(dc + 1) * 512]:
                                      e.matmul(o, l, r, start=False, stop=False),
                                      deps=mmdeps, inc=False)
                                tl = sc.op("t", lambda e, o=psx[:, :], l=ones_b[:, :],
                                           r=sb2_b[:, dc * 512:(dc + 1) * 512]:
                                           e.matmul(o, l, r, start=False, stop=True),
                                           deps=mmdeps)
                            rl = tok0 + tt * 128
                            if ei == 8:
                                bi = tt % 2
                                ldxc = sc.dma("y", "xb%d" % bi, lambda e, o=x1cb[bi][:, :],
                                              i=x1s[rl:rl + 128, dc * 512:(dc + 1) * 512]:
                                              e.dma_start(o, i), deps=x1cb_free[bi])
                                in1 = x1cb[bi][:, :]
                                vdeps = [("t", tl), ("xb%d" % bi, ldxc)]
                                if so_before:
                                    vdeps.append(("so", so_before))
                            else:
                                in1 = acc[:, tt, dc * 512:(dc + 1) * 512]
                                vdeps = [("t", tl), ("v", acc_v[tt][dc])]
                            va = sc.op("v", lambda e, o=acc[:, tt, dc * 512:(dc + 1) * 512],
                                       i0=psx[:, :], i1=in1:
                                       e.scalar_tensor_tensor(o, i0, 1.0, i1, ALU.mult, ALU.add),
                                       deps=vdeps)
                            fbank.consumed(j, "v", va)
                            acc_v[tt][dc] = va
                            if ei == 8:
                                x1cb_free[bi] = [("v", va)]
                            if ei == 9:
                                sc.dma("s", "so",
                                       lambda e, o=out[rl:rl + 128, dc * 512:(dc + 1) * 512],
                                       i=acc[:, tt, dc * 512:(dc + 1) * 512]:
                                       e.dma_start(o, i), deps=[("v", va)])
                        w2_free[wi] = [("t", sc.cnt["t"])]
                l2t_free = [("t", sc.cnt["t"])]

        sc.barrier()

        with ExitStack() as c2:
            a = c2.enter_context
            l2gT = [a(nc.sbuf_tensor("l2gT%d" % i, [128, NK, CAP], BF16)) for i in range(2)]
            hrawE = a(nc.sbuf_tensor("hrawE", [128, 8, CAP], BF16))
            out_sb = [a(nc.sbuf_tensor("out_sb%d" % i, [128, 5, 2048], BF16)) for i in range(2)]
            w1b2 = [a(nc.sbuf_tensor("w1c%d" % i, [128, NK, 512], BF16)) for i in range(2)]
            w2b2 = [a(nc.sbuf_tensor("w2c%d" % i, [128, 8, 512], BF16)) for i in range(2)]

            ltt_free = [[], []]
            w1_free2 = [[], []]
            w2_free2 = [[], []]
            out_free = [[[] for _ in range(5)] for _ in range(2)]
            w1_j = 0
            w2_j = 0

            so_full = ("so", sc.dslot.get("so", 0))

            def emit_lt(ei):
                par = ei % 2
                stores = bounce_done[ei]
                last = {}
                for k in range(NK):
                    q = "y" if k % 2 == 0 else "s"
                    slot = "lt%d%s" % (par, q)
                    last[slot] = sc.dma(q, slot, lambda e, k=k, par=par,
                                        o=l2gT[par][:, k, :], i=l2rg[par, :, k * 128:(k + 1) * 128]:
                                        e.dma_start_transpose(o, i),
                                        deps=stores + ltt_free[par])
                lrg_free[par] = [(s_, c_) for s_, c_ in last.items()]
                return [(s_, c_) for s_, c_ in last.items()]

            def emit_gather(ei):
                emit_bounce(ei)
                return emit_lt(ei)

            pend_ld = {}
            pend_ld[0] = emit_lt(0)
            pend_ld[1] = emit_lt(1)
            pending_scatter = None
            for ei in range(NE):
                par = ei % 2
                ldlast = pend_ld[ei]
                wi0 = w1_j % 2
                ldw1_0 = sc.dma("y", "wc1%d" % wi0, lambda e, wi0=wi0, ei=ei, o=w1b2[wi0][:, :, :],
                                i=w1s[ei, :, 0:512].rearrange("(k p) h -> p k h", p=128):
                                e.dma_start(o, i),
                                deps=w1_free2[wi0] + [("wcv%d" % ei, 2)])
                if pending_scatter is not None:
                    pending_scatter()
                    pending_scatter = None
                hdone = []
                for blk in range(2):
                    wi = w1_j % 2
                    w1_j += 1
                    if blk == 0:
                        ldw1 = ldw1_0
                    else:
                        ldw1 = sc.dma("y", "wc1%d" % wi, lambda e, wi=wi, ei=ei, blk=blk, o=w1b2[wi][:, :, :],
                                      i=w1s[ei, :, blk * 512:(blk + 1) * 512].rearrange("(k p) h -> p k h", p=128):
                                      e.dma_start(o, i),
                                      deps=w1_free2[wi] + [("wcv%d" % ei, 2)])
                    for hc2 in range(4):
                        hcf = blk * 4 + hc2
                        for (cs0, csw) in ((0, 512), (512, 128)):
                            j, psx, fdep = fbank.acquire()
                            mmdeps = [("wc1%d" % wi, ldw1)] + ldlast + ([fdep] if fdep else [])
                            tl = None
                            for k in range(NK):
                                tl = sc.op("t", lambda e, o=psx[:, :csw], l=w1b2[wi][:, k, hc2 * 128:(hc2 + 1) * 128],
                                           r=l2gT[par][:, k, cs0:cs0 + csw],
                                           st=(k == 0), sp_=(k == NK - 1):
                                           e.matmul(o, l, r, start=st, stop=sp_),
                                           deps=mmdeps, inc=(k == NK - 1))
                            sh = sc.op("s", lambda e, o=hrawE[:, hcf, cs0:cs0 + csw],
                                       i=psx[:, :csw], b=b1t_b[:, ei * 8 + hcf:ei * 8 + hcf + 1]:
                                       e.activation(o, i, AF.Gelu, bias=b), deps=[("t", tl)])
                            fbank.consumed(j, "s", sh)
                            hdone.append(sh)
                    w1_free2[wi] = [("t", sc.cnt["t"])]
                h_ready = [("s", max(hdone))]
                ltt_free[par] = [("t", sc.cnt["t"])]
                tsv = [[0] * 4 for _ in range(5)]
                for dc in range(4):
                    wi = w2_j % 2
                    w2_j += 1
                    ldw2 = sc.dma("y", "wc2%d" % wi, lambda e, o=w2b2[wi][:, :, :],
                                  i=w2s[ei, :, dc * 512:(dc + 1) * 512].rearrange("(k p) d -> p k d", p=128):
                                  e.dma_start(o, i),
                                  deps=w2_free2[wi] + [("wcv%d" % ei, 2)])
                    for t5 in range(5):
                        j, psx, fdep = fbank.acquire()
                        mmdeps = h_ready + [("wc2%d" % wi, ldw2)] + ([fdep] if fdep else [])
                        tl = None
                        for k in range(8):
                            tl = sc.op("t", lambda e, o=psx[:, :], l=hrawE[:, k, t5 * 128:(t5 + 1) * 128],
                                       r=w2b2[wi][:, k, :], st=(k == 0), sp_=(k == 7):
                                       e.matmul(o, l, r, start=st, stop=sp_),
                                       deps=mmdeps, inc=(k == 7))
                        vts = sc.op("v", lambda e, t5=t5, dc=dc,
                                    o=out_sb[par][:, t5, dc * 512:(dc + 1) * 512],
                                    i=psx[:, :], g=idxT[:, ei * 5 + t5, 1:2]:
                                    e.tensor_scalar(o, i, g, None, ALU.mult),
                                    deps=[("t", tl)] + (out_free[par][t5] if dc == 0 else []))
                        fbank.consumed(j, "v", vts)
                        tsv[t5][dc] = vts
                    w2_free2[wi] = [("t", sc.cnt["t"])]
                if ei + 2 < NE:
                    pend_ld[ei + 2] = emit_gather(ei + 2)

                def mk_scatter(ei=ei, tsv=tsv, par=par):
                    for t5 in range(5):
                        sca = sc.dma("g", "sca", lambda e, t5=t5, ei=ei, i=out_sb[par][:, t5, :]:
                                     e.indirect_dma_start(
                                         out, IndirectOffsetOnAxis(idx_i[:, ei * 5 + t5:ei * 5 + t5 + 1], 0),
                                         i, None, compute_op=ALU.add),
                                     deps=[("v", max(tsv[t5])), so_full,
                                           ("sca", sc.dslot.get("sca", 0))])
                        out_free[par][t5] = [("sca", sca)]
                pending_scatter = mk_scatter
            pending_scatter()

        sc.wait_only("y", [("so", sc.dslot.get("so", 0)),
                           ("sca", sc.dslot.get("sca", 0))])
        sc.wait_only("s", [("sca", sc.dslot.get("sca", 0))])
        sc.flush()

    return nc


def _host_inputs(inputs):
    x = np.ascontiguousarray(inputs["L0_attn_x"], dtype=np.float32)
    wqkv = np.asarray(inputs["L0_attn_W_qkv"], dtype=np.float32)
    shared = {
        "wq": np.ascontiguousarray(wqkv[:, 0::3]),
        "wk": np.ascontiguousarray(wqkv[:, 1::3]),
        "wv": np.ascontiguousarray(wqkv[:, 2::3]),
        "wo": np.ascontiguousarray(inputs["L0_attn_W_out"], dtype=np.float32),
        "bo": np.ascontiguousarray(inputs["L0_attn_b_out"], dtype=np.float32).reshape(1, D),
        "gw": np.ascontiguousarray(inputs["L0_ffn_gate_w"], dtype=np.float32),
        "b1t": np.ascontiguousarray(
            np.asarray(inputs["L0_ffn_b1"], dtype=np.float32).reshape(NE, 8, 128)
            .transpose(2, 0, 1).reshape(128, NE * 8)),
        "b2": np.ascontiguousarray(inputs["L0_ffn_b2"], dtype=np.float32),
        "w1": np.ascontiguousarray(inputs["L0_ffn_w1"], dtype=np.float32),
        "w2": np.ascontiguousarray(inputs["L0_ffn_w2"], dtype=np.float32),
        "sw1": np.ascontiguousarray(inputs["L0_ffn_shared_w1"], dtype=np.float32),
        "sb1t": np.ascontiguousarray(
            np.asarray(inputs["L0_ffn_shared_b1"], dtype=np.float32).reshape(16, 128).T),
        "sw2": np.ascontiguousarray(inputs["L0_ffn_shared_w2"], dtype=np.float32),
        "sb2": np.ascontiguousarray(
            inputs["L0_ffn_shared_b2"], dtype=np.float32).reshape(1, D),
        "rt": _rot_matrix(),
        "iotap1": np.broadcast_to(np.arange(1, CAP + 1, dtype=np.float32)[None, :],
                                  (128, CAP)).copy(),
        "tokv": np.ascontiguousarray(
            (np.arange(16, dtype=np.float32)[None, :] * 128
             + np.arange(128, dtype=np.float32)[:, None])),
        "idm": np.eye(128, dtype=ml_dtypes.bfloat16),
        "idmf": np.eye(128, dtype=np.float32),
        "onesb": np.ones((1, 128), dtype=ml_dtypes.bfloat16),
        "maskB": _mask_b(),
    }
    cores = []
    for c in range(8):
        b, r0 = c // 4, (c % 4) * TOK
        halo = (np.zeros((HALO, D), np.float32) if r0 == 0
                else x[b, r0 - HALO:r0])
        xh = np.ascontiguousarray(np.concatenate([halo, x[b, r0:r0 + TOK]], axis=0))
        pos = np.clip(np.arange(r0 - HALO, r0 + TOK), 0, None).astype(np.float64)
        inv = 1.0 / (10000.0 ** (np.arange(0, HD, 2, dtype=np.float64) / HD))
        ang = np.repeat(pos[:, None] * inv[None, :], 2, axis=1)
        core = dict(shared)
        core["xh"] = xh
        core["cost"] = np.ascontiguousarray(np.cos(ang).T.astype(np.float32))
        core["sint"] = np.ascontiguousarray(np.sin(ang).T.astype(np.float32))
        core["maskA"] = _mask_a() if r0 == 0 else shared["maskB"]
        cores.append(core)
    return cores


def _rot_matrix():
    r = np.zeros((128, 128), np.float32)
    for k in range(64):
        r[2 * k, 2 * k + 1] = -1.0
        r[2 * k + 1, 2 * k] = 1.0
    return r.T.astype(ml_dtypes.bfloat16)


def _mask_b():
    m = np.full((128, 256), NEG, np.float32)
    r = np.arange(128)
    m[:, :128][r[:, None] <= np.arange(128)[None, :]] = 0.0
    m[:, 128:][r[:, None] >= np.arange(128)[None, :]] = 0.0
    return m


def _mask_a():
    m = _mask_b().copy()
    m[:, :128] = NEG
    return m


def kernel(**inputs):
    cores = _host_inputs(inputs)
    nc = build_program()
    res = run_bass_kernel_spmd(nc, cores, core_ids=list(range(8)))
    outp = np.empty((2, 8192, D), np.float32)
    for c in range(8):
        b, r0 = c // 4, (c % 4) * TOK
        outp[b, r0:r0 + TOK] = res.results[c]["out"]
    return outp


# version 18
# speedup vs baseline: 1.0081x; 1.0081x over previous
import math
import numpy as np
import ml_dtypes
import concourse.bass as bass
import concourse.mybir as mybir
from concourse.bass_utils import run_bass_kernel_spmd

F32 = mybir.dt.float32
BF16 = mybir.dt.bfloat16
AF = mybir.ActivationFunctionType
ALU = mybir.AluOpType
AX = mybir.AxisListType

D = 2048
NH = 16
HD = 128
TOK = 2048
HALO = 128
NK = D // 128
SCALE = 1.0 / math.sqrt(HD)
NEG = -1.0e38
EPS = 1e-5
NE = 8
HID = 1024
SH = 2048
CAP = 640


class Sched:

    def __init__(self, nc, ctx):
        self.nc = nc
        self.ctx = ctx
        self.ops = {k: [] for k in ("t", "v", "s", "y", "g")}
        self.cnt = dict(t=0, v=0, s=0)
        self.dslot = {}
        self.last_wait = {q: {} for q in self.ops}
        self.sems = {}
        for ck in ("t", "v", "s"):
            self.sems[ck] = ctx.enter_context(nc.semaphore("sem_" + ck))

    def _dma_sem(self, slot):
        if slot not in self.sems:
            self.sems[slot] = self.ctx.enter_context(
                self.nc.semaphore("sem_" + slot))
            self.dslot[slot] = 0
        return self.sems[slot]

    def _waits(self, q, deps):
        ws = []
        for ck, val in deps:
            if val is None or val <= 0:
                continue
            if ck == "t" and q == "t":
                continue
            if val > self.last_wait[q].get(ck, 0):
                self.last_wait[q][ck] = val
                step = 1 if ck in ("t", "v", "s") else 16
                if ck not in self.sems:
                    self._dma_sem(ck)
                ws.append((self.sems[ck], val * step))
        return ws

    def op(self, q, fn, deps=(), inc=True):
        assert q in ("t", "v", "s")
        ws = self._waits(q, deps)
        sem = None
        if inc:
            self.cnt[q] += 1
            sem = self.sems[q]

        def emit(eng, ws=ws, fn=fn, sem=sem):
            for s_, v_ in ws:
                eng.wait_ge(s_, v_)
            inst = fn(eng)
            if sem is not None:
                inst.then_inc(sem, 1)

        self.ops[q].append(emit)
        return self.cnt[q]

    def dma(self, q, slot, fn, deps=()):
        sem = self._dma_sem(slot)
        ws = self._waits(q, deps)
        self.dslot[slot] += 1

        def emit(eng, ws=ws, fn=fn, sem=sem):
            for s_, v_ in ws:
                eng.wait_ge(s_, v_)
            fn(eng).then_inc(sem, 16)

        self.ops[q].append(emit)
        return self.dslot[slot]

    def wait_only(self, q, deps):
        ws = self._waits(q, deps)
        if not ws:
            return

        def emit(eng, ws=ws):
            for s_, v_ in ws:
                eng.wait_ge(s_, v_)

        self.ops[q].append(emit)

    def barrier(self):
        snap = dict(self.cnt)
        snap.update(self.dslot)
        for q in self.ops:
            deps = [(ck, v) for ck, v in snap.items() if v > 0 and ck != q]
            self.wait_only(q, deps)

    def flush(self):
        with self.nc.Block() as block:
            for q, eng_name in (("t", "tensor"), ("v", "vector"),
                                ("s", "scalar"), ("y", "sync"), ("g", "gpsimd")):
                ops = self.ops[q]
                if not ops:
                    continue
                getattr(block, eng_name)(lambda e, ops=ops: [f(e) for f in ops])
        self.ops = {k: [] for k in self.ops}


class Banks:

    def __init__(self, tensors):
        self.t = tensors
        self.free_dep = [None] * len(tensors)
        self.i = 0

    def acquire(self):
        j = self.i % len(self.t)
        self.i += 1
        return j, self.t[j], self.free_dep[j]

    def consumed(self, j, ck, count):
        self.free_dep[j] = (ck, count)


def _ln_a(sc, xt_ap, st, deps):
    v1 = sc.op("v", lambda e: e.reduce_sum(st["ssum"], xt_ap, axis=AX.X),
               deps=deps + st["war"])
    s1 = sc.op("s", lambda e: e.activation(st["sq"], xt_ap, AF.Square,
                                           accum_out=st["ssq"]),
               deps=deps + st["war"] + [("s", sc.cnt["s"])])
    st["a"] = (v1, s1)


def _ln_b(sc, xt_ap, st, out_bf_ap, out_deps=()):
    v1, s1 = st["a"]
    ssum, ssq, mean, m2, var, rstd = (st["ssum"], st["ssq"], st["mean"],
                                      st["m2"], st["var"], st["rstd"])
    v3 = sc.op("v", lambda e: e.tensor_scalar(mean, ssum, 1.0 / D, None, ALU.mult),
               deps=[("v", v1)])
    v5 = sc.op("v", lambda e: e.tensor_mul(m2, mean, mean), deps=[("v", v3)])
    v6 = sc.op("v", lambda e: e.scalar_tensor_tensor(var, ssq, 1.0 / D, m2,
                                                     ALU.mult, ALU.subtract),
               deps=[("s", s1), ("v", v5)])
    v7 = sc.op("v", lambda e: e.tensor_scalar(var, var, EPS, None, ALU.add),
               deps=[("v", v6)])
    s2 = sc.op("s", lambda e: e.activation(var, var, AF.Ln), deps=[("v", v7)])
    s3 = sc.op("s", lambda e: e.activation(rstd, var, AF.Exp, scale=-0.5),
               deps=[("s", s2)])
    s4 = sc.op("v", lambda e: e.tensor_scalar(out_bf_ap, xt_ap, mean, rstd,
                                              ALU.subtract, ALU.mult),
               deps=[("s", s3)] + list(out_deps))
    st["war"] = [("v", s4), ("s", s3)]
    return s4


def build_program(debug=False):
    nc = bass.Bass("TRN2", target_bir_lowering=False)

    xh = nc.dram_tensor("xh", [HALO + TOK, D], F32, kind="ExternalInput").ap()
    wq = nc.dram_tensor("wq", [D, D], F32, kind="ExternalInput").ap()
    wk = nc.dram_tensor("wk", [D, D], F32, kind="ExternalInput").ap()
    wv = nc.dram_tensor("wv", [D, D], F32, kind="ExternalInput").ap()
    wo = nc.dram_tensor("wo", [D, D], F32, kind="ExternalInput").ap()
    bo = nc.dram_tensor("bo", [1, D], F32, kind="ExternalInput").ap()
    cost = nc.dram_tensor("cost", [128, HALO + TOK], F32, kind="ExternalInput").ap()
    sint = nc.dram_tensor("sint", [128, HALO + TOK], F32, kind="ExternalInput").ap()
    rt = nc.dram_tensor("rt", [128, 128], BF16, kind="ExternalInput").ap()
    idm = nc.dram_tensor("idm", [128, 128], BF16, kind="ExternalInput").ap()
    idmf = nc.dram_tensor("idmf", [128, 128], F32, kind="ExternalInput").ap()
    onesb = nc.dram_tensor("onesb", [1, 128], BF16, kind="ExternalInput").ap()
    maskA = nc.dram_tensor("maskA", [128, 256], F32, kind="ExternalInput").ap()
    maskB = nc.dram_tensor("maskB", [128, 256], F32, kind="ExternalInput").ap()
    gw = nc.dram_tensor("gw", [D, NE], F32, kind="ExternalInput").ap()
    b1t = nc.dram_tensor("b1t", [128, NE * 8], F32, kind="ExternalInput").ap()
    b2 = nc.dram_tensor("b2", [NE, D], F32, kind="ExternalInput").ap()
    w1 = nc.dram_tensor("w1", [NE, D, HID], F32, kind="ExternalInput").ap()
    w2 = nc.dram_tensor("w2", [NE, HID, D], F32, kind="ExternalInput").ap()
    sw1 = nc.dram_tensor("sw1", [D, SH], F32, kind="ExternalInput").ap()
    sb1t = nc.dram_tensor("sb1t", [128, 16], F32, kind="ExternalInput").ap()
    sw2 = nc.dram_tensor("sw2", [SH, D], F32, kind="ExternalInput").ap()
    sb2 = nc.dram_tensor("sb2", [1, D], F32, kind="ExternalInput").ap()
    iotap1 = nc.dram_tensor("iotap1", [128, CAP], F32, kind="ExternalInput").ap()
    tokv = nc.dram_tensor("tokv", [128, 16], F32, kind="ExternalInput").ap()

    x1s = nc.dram_tensor("x1s", [TOK, D], F32, kind="Internal").ap()
    l2s = nc.dram_tensor("l2s", [D, TOK], BF16, kind="Internal").ap()
    l2r = nc.dram_tensor("l2r", [TOK, D], BF16, kind="Internal").ap()
    l2rg = nc.dram_tensor("l2rg", [2, CAP, D], BF16, kind="Internal").ap()
    w1s = nc.dram_tensor("w1s", [NE, D, HID], BF16, kind="Internal").ap()
    w2s = nc.dram_tensor("w2s", [NE, HID, D], BF16, kind="Internal").ap()
    out = nc.dram_tensor("out", [TOK, D], F32, kind="ExternalOutput").ap()
    if debug:
        dbg_x1 = nc.dram_tensor("dbg_x1", [TOK, D], F32, kind="ExternalOutput").ap()
        dbg_l2 = nc.dram_tensor("dbg_l2", [D, TOK], BF16, kind="ExternalOutput").ap()
        dbg_g = nc.dram_tensor("dbg_g", [128, 16 * NE], F32, kind="ExternalOutput").ap()

    from contextlib import ExitStack
    with ExitStack() as ctx:
        sc = Sched(nc, ctx)

        ob = ctx.enter_context
        rt_b = ob(nc.sbuf_tensor("rt_b", [128, 128], BF16))
        idm_b = ob(nc.sbuf_tensor("idm_b", [128, 128], BF16))
        idmf_b = ob(nc.sbuf_tensor("idmf_b", [128, 128], F32))
        ones_b = ob(nc.sbuf_tensor("ones_b", [1, 128], BF16))
        mA_b = ob(nc.sbuf_tensor("mA_b", [128, 256], F32))
        mB_b = ob(nc.sbuf_tensor("mB_b", [128, 256], F32))
        bo_b = ob(nc.sbuf_tensor("bo_b", [1, D], BF16))
        gw_b = ob(nc.sbuf_tensor("gw_b", [128, NK, NE], BF16))
        b1t_b = ob(nc.sbuf_tensor("b1t_b", [128, NE * 8], F32))
        sb1t_b = ob(nc.sbuf_tensor("sb1t_b", [128, 16], F32))
        b2_b = ob(nc.sbuf_tensor("b2_b", [NE, D], BF16))
        sb2_b = ob(nc.sbuf_tensor("sb2_b", [1, D], BF16))
        gates = ob(nc.sbuf_tensor("gates", [128, 16 * NE], F32))
        gatesT = ob(nc.sbuf_tensor("gatesT", [NE, 16 * 128], BF16))
        idxT = ob(nc.sbuf_tensor("idxT", [128, NE * 5, 4], F32))
        idx_i = ob(nc.sbuf_tensor("idx_i", [128, NE * 5], mybir.dt.int32))


        g_sb = [ob(nc.sbuf_tensor("g_sb%d" % i, [128, D], BF16)) for i in range(2)]
        pf = [ob(nc.psum_tensor("pf%d" % i, [128, 512], F32)) for i in range(6)]
        pb = [ob(nc.psum_tensor("pb%d" % i, [128, 8, 128], BF16)) for i in range(2)]
        fbank = Banks(pf)
        bbank = Banks(pb)

        sc.dma("y", "yc", lambda e: e.dma_start(rt_b[:, :], rt))
        sc.dma("y", "yc", lambda e: e.dma_start(idm_b[:, :], idm))
        sc.dma("y", "yc", lambda e: e.dma_start(idmf_b[:, :], idmf))
        sc.dma("y", "yc", lambda e: e.dma_start(ones_b[:, :], onesb))
        sc.dma("y", "yc", lambda e: e.dma_start(mA_b[:, :], maskA))
        sc.dma("y", "yc", lambda e: e.dma_start(mB_b[:, :], maskB))
        sc.dma("y", "yc", lambda e: e.dma_start(b1t_b[:, :], b1t))
        ld_const_y = sc.dma("y", "yc", lambda e: e.dma_start(sb1t_b[:, :], sb1t))
        sc.dma("g", "gc", lambda e: e.dma_start(bo_b[:, :], bo))
        sc.dma("g", "gc", lambda e: e.dma_start(
            gw_b[:, :, :], gw.rearrange("(k p) e -> p k e", p=128)))
        sc.dma("g", "gc", lambda e: e.dma_start(b2_b[:, :], b2))
        ld_const_g = sc.dma("g", "gc", lambda e: e.dma_start(sb2_b[:, :], sb2))

        x1_store = {}
        l2_store_last = 0

        conv_jobs = []
        for cei in range(NE):
            conv_jobs.append((w1s[cei, :, :], w1[cei, :, :], cei))
            conv_jobs.append((w2s[cei, :, :], w2[cei, :, :], cei))
        conv_state = {"i": 0}

        def emit_conv(n=1):
            for _ in range(n):
                if conv_state["i"] >= len(conv_jobs):
                    return
                dstap, srcap, cei = conv_jobs[conv_state["i"]]
                conv_state["i"] += 1
                sc.dma("g", "wcv%d" % cei, lambda e, o=dstap, i=srcap:
                       e.dma_start(o, i), deps=[])

        abw = ExitStack()
        cos_b = abw.enter_context(nc.sbuf_tensor("cos_b", [128, HALO + TOK], BF16))
        sin_b = abw.enter_context(nc.sbuf_tensor("sin_b", [128, HALO + TOK], BF16))
        sc.dma("g", "gc", lambda e: e.dma_start(cos_b[:, :], cost))
        ld_const_g = sc.dma("g", "gc", lambda e: e.dma_start(sin_b[:, :], sint))
        x1_store = {}
        for sp in range(2):
            from contextlib import ExitStack as _ES
            with _ES() as spsc:
                a = spsc.enter_context
                lnxT = a(nc.sbuf_tensor("lnxT%d" % sp, [128, NK, HALO + 1024], BF16))
                attnT = a(nc.sbuf_tensor("attnT%d" % sp, [128, NH, 1024], BF16))
                base = sp * 1024

                with _ES() as asc:
                    aa = asc.enter_context
                    xt = [aa(nc.sbuf_tensor("xt%d_%d" % (sp, i), [128, D], F32)) for i in range(3)]
                    sqs = aa(nc.sbuf_tensor("sqs%d" % sp, [128, D], F32))
                    lnb_bufs = [aa(nc.sbuf_tensor("lnb%d_%d" % (sp, i), [128, D], BF16)) for i in range(2)]
                    stats = {}
                    for nm in ("ssum", "ssq", "mean", "m2", "var", "rstd"):
                        stats[nm] = [aa(nc.sbuf_tensor("stA%d_%s%d" % (sp, nm, i), [128, 1], F32))
                                     for i in range(3)]
                    xt_free = [[] for _ in range(3)]
                    lnb_free = [[] for _ in range(2)]
                    st_war = [{"war": []} for _ in range(3)]
                    lnxT_evac = []

                    def ln_finish(tt):
                        bi = tt % 3
                        li = tt % 2
                        lnb = lnb_bufs[li]
                        st = st_war[bi]
                        s4 = _ln_b(sc, xt[bi][:, :], st, lnb[:, :], out_deps=lnb_free[li])
                        xt_free[bi] = [("v", s4)]
                        evs = []
                        for g2 in range(2):
                            j, pbx, fdep = bbank.acquire()
                            tdeps = [("v", s4), ("yc", ld_const_y)] + ([fdep] if fdep else [])
                            tl = None
                            for k2 in range(8):
                                k = g2 * 8 + k2
                                tl = sc.op("t", lambda e, o=pbx[:, k2, :],
                                           i=lnb[:, k * 128:(k + 1) * 128], idq=idm_b[:, :]:
                                           e.transpose(o, i, idq),
                                           deps=tdeps if k2 == 0 else [("v", s4)],
                                           inc=(k2 == 7))
                            ev = sc.op("s", lambda e, o=lnxT[:, g2 * 8:(g2 + 1) * 8, tt * 128:(tt + 1) * 128],
                                       i=pbx[:, :, :]: e.copy(o, i), deps=[("t", tl)])
                            bbank.consumed(j, "s", ev)
                            evs.append(ev)
                        lnb_free[li] = [("t", sc.cnt["t"])]
                        lnxT_evac.append(evs[-1])

                    for tt in range(9):
                        bi = tt % 3
                        r0 = base + tt * 128
                        ldx = sc.dma("y", "xt%d" % bi, lambda e, o=xt[bi][:, :], i=xh[r0:r0 + 128, :]:
                                     e.dma_start(o, i), deps=xt_free[bi])
                        st = st_war[bi]
                        for nm in stats:
                            st[nm] = stats[nm][bi][:, :]
                        st["sq"] = sqs[:, :]
                        _ln_a(sc, xt[bi][:, :], st, deps=[("xt%d" % bi, ldx)])
                        if tt >= 1:
                            ln_finish(tt - 1)
                    ln_finish(8)
                    lnxT_all = [("s", max(lnxT_evac))]
                sc.barrier()

                with _ES() as bsc:
                    bb = bsc.enter_context
                    QT = bb(nc.sbuf_tensor("QT%d" % sp, [128, 1024], BF16))
                    KT = bb(nc.sbuf_tensor("KT%d" % sp, [128, HALO + 1024], BF16))
                    qb_t = [bb(nc.sbuf_tensor("qb_t%d_%d" % (sp, i), [128, 512], BF16)) for i in range(5)]
                    qs_t = [bb(nc.sbuf_tensor("qs_t%d_%d" % (sp, i), [128, 512], BF16)) for i in range(5)]
                    qc_t = [bb(nc.sbuf_tensor("qc_t%d_%d" % (sp, i), [128, 512], BF16)) for i in range(5)]
                    rope_free = [[] for _ in range(5)]
                    v4 = [bb(nc.sbuf_tensor("v4_%d_%d" % (sp, i), [128, 9, 512], BF16)) for i in range(2)]
                    wq_b = [bb(nc.sbuf_tensor("wq_b%d_%d" % (sp, i), [128, NK, 128], BF16)) for i in range(2)]
                    wk_b = [bb(nc.sbuf_tensor("wk_b%d_%d" % (sp, i), [128, NK, 128], BF16)) for i in range(2)]
                    wv_b = [bb(nc.sbuf_tensor("wv_b%d_0" % sp, [128, NK, 512], BF16))]
                    Pf = [bb(nc.sbuf_tensor("Pf%d_%d" % (sp, i), [128, 256], F32)) for i in range(8)]
                    Pf2 = [bb(nc.sbuf_tensor("Pf2%d_%d" % (sp, i), [128, 256], BF16)) for i in range(8)]
                    Pbf = [[bb(nc.sbuf_tensor("Pbf%d_%d_%d" % (sp, pp, i), [128, 256], BF16))
                            for i in range(8)] for pp in range(2)]
                    PTb = [bb(nc.sbuf_tensor("PT%d_%d" % (sp, i), [128, 2, 128], BF16)) for i in range(8)]
                    smx = bb(nc.sbuf_tensor("smx%d" % sp, [128, 32], F32))
                    wq_free = [[], []]
                    wk_free = [[], []]
                    wv_free = [[]]
                    Pf_free = [[] for _ in range(8)]
                    Pf2_free = [[] for _ in range(8)]
                    Pbf_free = [[[] for _ in range(8)] for _ in range(2)]
                    PT_free = [[] for _ in range(8)]
                    v4_ready = [[], []]
                    ptav_free = [([bbank.free_dep[i]] if bbank.free_dep[i] else [])
                                 for i in range(2)]

                    def emit_ptav(ph, pvnrs, pgvi):
                        ppar = ph % 2
                        spts = [None] * 8
                        glast = [None, None]
                        for g in range(2):
                            pbx = pb[g]
                            tp = None
                            for qi in range(4):
                                qt = g * 4 + qi
                                sl = qi * 2
                                for half in range(2):
                                    tp = sc.op("t", lambda e, o=pbx[:, sl + half, :],
                                               i=Pbf[ppar][qt][:, half * 128:(half + 1) * 128], idq=idm_b[:, :]:
                                               e.transpose(o, i, idq),
                                               deps=[("s", pvnrs[qt])] + (ptav_free[g] if qi == 0 and half == 0 else []),
                                               inc=(qi == 3 and half == 1))
                                Pbf_free[ppar][qt] = [("t", tp)]
                            glast[g] = tp
                        for g in range(2):
                            pbx = pb[g]
                            last_spt = None
                            for qi in range(4):
                                qt = g * 4 + qi
                                sl = qi * 2
                                last_spt = sc.op("s", lambda e, o=PTb[qt][:, :, :], i=pbx[:, sl:sl + 2, :]:
                                                 e.copy(o, i), deps=[("t", glast[g])] + PT_free[qt])
                                spts[qt] = last_spt
                            ptav_free[g] = [("s", last_spt)]
                        for qt in range(8):
                            j2, psav, fdep2 = fbank.acquire()
                            hx = ph % 4
                            sc.op("t", lambda e, o=psav[:, :128], l=v4[pgvi][:, qt, hx * 128:(hx + 1) * 128],
                                  r=PTb[qt][:, 0, :]: e.matmul(o, l, r, start=True, stop=False),
                                  deps=v4_ready[pgvi] + [("s", spts[qt])] + ([fdep2] if fdep2 else []), inc=False)
                            ta = sc.op("t", lambda e, o=psav[:, :128], l=v4[pgvi][:, qt + 1, hx * 128:(hx + 1) * 128],
                                       r=PTb[qt][:, 1, :]: e.matmul(o, l, r, start=False, stop=True),
                                       deps=[("s", spts[qt])])
                            PT_free[qt] = [("t", ta)]
                            sae = sc.op("s", lambda e, o=attnT[:, ph, qt * 128:(qt + 1) * 128],
                                        i=psav[:, :128]: e.activation(o, i, AF.Copy), deps=[("t", ta)])
                            fbank.consumed(j2, "s", sae)

                    pend = None
                    for h in range(NH):
                        emit_conv(1)
                        wbi = h % 2
                        ldq = sc.dma("g", "wq%d" % wbi, lambda e, o=wq_b[wbi][:, :, :],
                                     i=wq[:, h * 128:(h + 1) * 128].rearrange("(k p) d -> p k d", p=128):
                                     e.dma_start(o, i), deps=wq_free[wbi])
                        ldk = sc.dma("g", "wk%d" % wbi, lambda e, o=wk_b[wbi][:, :, :],
                                     i=wk[:, h * 128:(h + 1) * 128].rearrange("(k p) d -> p k d", p=128):
                                     e.dma_start(o, i), deps=wk_free[wbi])
                        gvi = 0
                        vpar = (h // 4) % 2
                        if h % 4 == 0:
                            ldv = sc.dma(
                                "g", "wv%d" % gvi, lambda e, o=wv_b[gvi][:, :, :],
                                i=wv[:, h * 128:(h + 4) * 128].rearrange("(k p) d -> p k d", p=128):
                                e.dma_start(o, i), deps=wv_free[gvi])
                            v_ev = []
                            for t9 in range(9):
                                j, psx, fdep = fbank.acquire()
                                mmdeps = lnxT_all + [("wv%d" % gvi, ldv)] + ([fdep] if fdep else [])
                                tl = None
                                for k in range(NK):
                                    tl = sc.op("t", lambda e, o=psx[:, :], l=lnxT[:, k, t9 * 128:(t9 + 1) * 128],
                                               r=wv_b[gvi][:, k, :], st=(k == 0), sp_=(k == NK - 1):
                                               e.matmul(o, l, r, start=st, stop=sp_),
                                               deps=mmdeps, inc=(k == NK - 1))
                                ev = sc.op("s", lambda e, o=v4[vpar][:, t9, :], i=psx[:, :]:
                                           e.activation(o, i, AF.Copy), deps=[("t", tl)])
                                fbank.consumed(j, "s", ev)
                                v_ev.append(ev)
                            v4_ready[vpar] = [("s", max(v_ev))]
                            wv_free[gvi] = [("t", sc.cnt["t"])]

                        chunks = ([(QT, 128, c0, cw, wq_b[wbi], "wq%d" % wbi, ldq)
                                   for (c0, cw) in [(128, 512), (640, 512)]] +
                                  [(KT, 0, c0, cw, wk_b[wbi], "wk%d" % wbi, ldk)
                                   for (c0, cw) in [(0, 512), (512, 512), (1024, 128)]])
                        ses = []
                        for ci, (dst, off0, c0, cw, wbuf, wslot, ldw) in enumerate(chunks):
                            j, psx, fdep = fbank.acquire()
                            mmdeps = lnxT_all + [(wslot, ldw)] + ([fdep] if fdep else [])
                            tl = None
                            for k in range(NK):
                                tl = sc.op("t", lambda e, o=psx[:, :cw], l=wbuf[:, k, :],
                                           r=lnxT[:, k, c0:c0 + cw], st=(k == 0), sp_=(k == NK - 1):
                                           e.matmul(o, l, r, start=st, stop=sp_),
                                           deps=mmdeps, inc=(k == NK - 1))
                            se = sc.op("s", lambda e, o=qb_t[ci][:, :cw], i=psx[:, :cw]:
                                       e.activation(o, i, AF.Copy),
                                       deps=[("t", tl)] + rope_free[ci])
                            fbank.consumed(j, "s", se)
                            ses.append(se)
                        vqs = []
                        for ci, (dst, off0, c0, cw, wbuf, wslot, ldw) in enumerate(chunks):
                            gc0 = base + c0
                            vqs.append(sc.op("v", lambda e, o=qs_t[ci][:, :cw], i0=qb_t[ci][:, :cw],
                                             i1=sin_b[:, gc0:gc0 + cw]:
                                             e.tensor_mul(o, i0, i1),
                                             deps=[("s", ses[ci]), ("gc", ld_const_g)]))
                        vcs = []
                        for ci, (dst, off0, c0, cw, wbuf, wslot, ldw) in enumerate(chunks):
                            gc0 = base + c0
                            vcs.append(sc.op("v", lambda e, o=qc_t[ci][:, :cw], i0=qb_t[ci][:, :cw],
                                             i1=cos_b[:, gc0:gc0 + cw]:
                                             e.tensor_mul(o, i0, i1), deps=[("s", ses[ci])]))
                        trs = []
                        for ci, (dst, off0, c0, cw, wbuf, wslot, ldw) in enumerate(chunks):
                            j2, psr, fdep2 = fbank.acquire()
                            trs.append((j2, psr, sc.op(
                                "t", lambda e, o=psr[:, :cw], l=rt_b[:, :], r=qs_t[ci][:, :cw]:
                                e.matmul(o, l, r, start=True, stop=True),
                                deps=[("v", vqs[ci]), ("yc", ld_const_y)] + ([fdep2] if fdep2 else []))))
                        vfs = []
                        for ci, (dst, off0, c0, cw, wbuf, wslot, ldw) in enumerate(chunks):
                            j2, psr, tr = trs[ci]
                            vf = sc.op("v", lambda e, o=dst[:, c0 - off0:c0 - off0 + cw],
                                       i0=qc_t[ci][:, :cw], i1=psr[:, :cw]:
                                       e.tensor_add(o, i0, i1), deps=[("v", vcs[ci]), ("t", tr)])
                            fbank.consumed(j2, "v", vf)
                            rope_free[ci] = [("v", vf), ("t", tr)]
                            vfs.append(vf)
                        wq_free[wbi] = [("t", sc.cnt["t"])]
                        wk_free[wbi] = [("t", sc.cnt["t"])]
                        qk_ready = [("v", max(vfs))]

                        vms, vmxs, sxps, vsms, vrcs = [], [], [], [], []
                        for qt in range(8):
                            j, psx, fdep = fbank.acquire()
                            ts_ = sc.op("t", lambda e, o=psx[:, :256], l=QT[:, qt * 128:(qt + 1) * 128],
                                        r=KT[:, qt * 128:qt * 128 + 256]:
                                        e.matmul(o, l, r, start=True, stop=True),
                                        deps=qk_ready + ([fdep] if fdep else []))
                            mask = mA_b if (sp == 0 and qt == 0) else mB_b
                            vm = sc.op("v", lambda e, o=Pf[qt][:, :], i=psx[:, :256], m=mask[:, :]:
                                       e.scalar_tensor_tensor(o, i, SCALE, m, ALU.mult, ALU.add),
                                       deps=[("t", ts_), ("yc", ld_const_y)] + Pf_free[qt])
                            fbank.consumed(j, "v", vm)
                            vms.append(vm)
                        for qt in range(8):
                            vmxs.append(sc.op("v", lambda e, o=smx[:, qt * 4:qt * 4 + 1], i=Pf[qt][:, :]:
                                              e.reduce_max(o, i, axis=AX.X, negate=True),
                                              deps=[("v", vms[qt])]))
                        for qt in range(8):
                            sxps.append(sc.op("s", lambda e, o=Pf2[qt][:, :], i=Pf[qt][:, :],
                                              b=smx[:, qt * 4:qt * 4 + 1]:
                                              e.activation(o, i, AF.Exp, bias=b),
                                              deps=[("v", vmxs[qt])] + Pf2_free[qt]))
                        for qt in range(8):
                            vsms.append(sc.op("v", lambda e, o=smx[:, qt * 4 + 1:qt * 4 + 2], i=Pf2[qt][:, :]:
                                              e.reduce_sum(o, i, axis=AX.X), deps=[("s", sxps[qt])]))
                        lns = []
                        for qt in range(8):
                            lns.append(sc.op("s", lambda e, o=smx[:, qt * 4 + 2:qt * 4 + 3],
                                             i=smx[:, qt * 4 + 1:qt * 4 + 2]:
                                             e.activation(o, i, AF.Ln), deps=[("v", vsms[qt])]))
                        vrcs = []
                        for qt in range(8):
                            vrcs.append(sc.op("v", lambda e, o=smx[:, qt * 4 + 2:qt * 4 + 3],
                                              i0=smx[:, qt * 4:qt * 4 + 1],
                                              i1=smx[:, qt * 4 + 2:qt * 4 + 3]:
                                              e.tensor_sub(o, i0, i1),
                                              deps=[("s", lns[qt])]))
                        vnrs = []
                        for qt in range(8):
                            vnr = sc.op("s", lambda e, o=Pbf[h % 2][qt][:, :], i=Pf[qt][:, :],
                                        b=smx[:, qt * 4 + 2:qt * 4 + 3]:
                                        e.activation(o, i, AF.Exp, bias=b),
                                        deps=[("v", vrcs[qt])] + Pbf_free[h % 2][qt])
                            Pf_free[qt] = [("s", vnr)]
                            vnrs.append(vnr)
                        if pend is not None:
                            emit_ptav(*pend)
                        pend = (h, vnrs, vpar)

                    emit_ptav(*pend)
                    bbank.consumed(0, "s", sc.cnt["s"])
                    bbank.consumed(1, "s", sc.cnt["s"])
                    attnT_all = [("s", sc.cnt["s"])]
                sc.barrier()

                with _ES() as b3sc:
                    b3 = b3sc.enter_context
                    wo_b = [b3(nc.sbuf_tensor("wo_b%d_%d" % (sp, i), [128, NK, 512], BF16)) for i in range(2)]
                    xc_b = [b3(nc.sbuf_tensor("xc_b%d_%d" % (sp, i), [128, 512], F32)) for i in range(2)]
                    x1c = [b3(nc.sbuf_tensor("x1c%d_%d" % (sp, i), [128, 512], F32)) for i in range(2)]
                    x1t = [b3(nc.sbuf_tensor("x1t%d_%d" % (sp, i), [128, D], F32)) for i in range(3)]
                    sqsB = b3(nc.sbuf_tensor("sqB%d" % sp, [128, D], F32))
                    ln2b = [b3(nc.sbuf_tensor("ln2b%d_%d" % (sp, i), [128, D], BF16))
                            for i in range(2)]
                    l2tb = [b3(nc.sbuf_tensor("l2tb%d_%d" % (sp, i), [128, NK, 128], BF16)) for i in range(2)]
                    statsB = {}
                    for nm in ("ssum", "ssq", "mean", "m2", "var", "rstd"):
                        statsB[nm] = [b3(nc.sbuf_tensor("stB%d_%s%d" % (sp, nm, i), [128, 1], F32))
                                      for i in range(3)]
                    gsc = {nm: b3(nc.sbuf_tensor("g%d_%s" % (sp, nm),
                                                 [128, NE if nm in ("lg", "im", "mk", "e1") else 1], F32))
                           for nm in ("lg", "im", "mk", "e1", "nm1", "nm2", "den", "rcd")}
                    gbf = b3(nc.sbuf_tensor("gbf%d" % sp, [128, NE], BF16))
                    wo_free = [[], []]
                    xc_free = [[], []]
                    x1c_free = [[], []]
                    x1t_free = [[], [], []]
                    l2tb_free = [[], []]
                    ln2b_free = [[], []]
                    stB_war = [{"war": []} for _ in range(3)]

                    for dc in range(4):
                        emit_conv(1)
                        wbi = dc % 2
                        ldw = sc.dma("g", "wo%d" % wbi, lambda e, o=wo_b[wbi][:, :, :],
                                     i=wo[:, dc * 512:(dc + 1) * 512].rearrange("(k p) d -> p k d", p=128):
                                     e.dma_start(o, i), deps=wo_free[wbi])
                        for tt in range(8):
                            bi = tt % 2
                            j, psx, fdep = fbank.acquire()
                            mmdeps = attnT_all + [("wo%d" % wbi, ldw), ("gc", ld_const_g),
                                                  ("yc", ld_const_y)] + ([fdep] if fdep else [])
                            for k in range(NK):
                                sc.op("t", lambda e, o=psx[:, :], l=attnT[:, k, tt * 128:(tt + 1) * 128],
                                      r=wo_b[wbi][:, k, :], st=(k == 0):
                                      e.matmul(o, l, r, start=st, stop=False),
                                      deps=mmdeps, inc=False)
                            tl = sc.op("t", lambda e, o=psx[:, :], l=ones_b[:, :],
                                       r=bo_b[:, dc * 512:(dc + 1) * 512]:
                                       e.matmul(o, l, r, start=False, stop=True), deps=mmdeps)
                            r0 = HALO + sp * 1024 + tt * 128
                            ldxc = sc.dma("y", "xc%d" % bi, lambda e, o=xc_b[bi][:, :], i=xh[r0:r0 + 128, dc * 512:(dc + 1) * 512]:
                                          e.dma_start(o, i), deps=xc_free[bi])
                            vx = sc.op("v", lambda e, o=x1c[bi][:, :], i0=psx[:, :], i1=xc_b[bi][:, :]:
                                       e.tensor_add(o, i0, i1),
                                       deps=[("t", tl), ("xc%d" % bi, ldxc)] + x1c_free[bi])
                            fbank.consumed(j, "v", vx)
                            xc_free[bi] = [("v", vx)]
                            rl = sp * 1024 + tt * 128
                            st1 = sc.dma("s", "sx%d" % bi, lambda e, o=x1s[rl:rl + 128, dc * 512:(dc + 1) * 512],
                                         i=x1c[bi][:, :]: e.dma_start(o, i), deps=[("v", vx)])
                            x1c_free[bi] = [("sx%d" % bi, st1)]
                            x1_store[(sp, tt, dc)] = st1
                        wo_free[wbi] = [("t", sc.cnt["t"])]

                    def p2_finish(tt):
                        bi = tt % 3
                        li = tt % 2
                        rl = sp * 1024 + tt * 128
                        gt_g = sp * 8 + tt
                        st = stB_war[bi]
                        s4 = _ln_b(sc, x1t[bi][:, :], st, ln2b[li][:, :],
                                   out_deps=ln2b_free[li])
                        x1t_free[bi] = [("v", s4)]
                        str_ = sc.dma("s", "sr%d" % li, lambda e, o=l2r[rl:rl + 128, :],
                                      i=ln2b[li][:, :]: e.dma_start(o, i), deps=[("v", s4)])
                        evs = []
                        for g2 in range(2):
                            j, pbx, fdep = bbank.acquire()
                            tdeps = [("v", s4)] + ([fdep] if fdep else [])
                            tl = None
                            for k2 in range(8):
                                k = g2 * 8 + k2
                                tl = sc.op("t", lambda e, o=pbx[:, k2, :],
                                           i=ln2b[li][:, k * 128:(k + 1) * 128], idq=idm_b[:, :]:
                                           e.transpose(o, i, idq),
                                           deps=tdeps if k2 == 0 else [("v", s4)],
                                           inc=(k2 == 7))
                            ev = sc.op("s", lambda e, o=l2tb[li][:, g2 * 8:(g2 + 1) * 8, :], i=pbx[:, :, :]:
                                       e.copy(o, i), deps=[("t", tl)] + (l2tb_free[li] if g2 == 0 else []))
                            bbank.consumed(j, "s", ev)
                            evs.append(ev)
                        ln2b_free[li] = [("t", sc.cnt["t"]), ("sr%d" % li, str_)]
                        stl = sc.dma("s", "sl%d" % li, lambda e, o=l2s.rearrange("(k p) t -> p k t", p=128)[:, :, rl:rl + 128],
                                     i=l2tb[li][:, :, :]: e.dma_start(o, i), deps=[("s", max(evs))])
                        j, psx, fdep = fbank.acquire()
                        tl = None
                        for k in range(NK):
                            tl = sc.op("t", lambda e, o=psx[:, :NE], l=l2tb[li][:, k, :],
                                       r=gw_b[:, k, :], st=(k == 0), sp_=(k == NK - 1):
                                       e.matmul(o, l, r, start=st, stop=sp_),
                                       deps=[("s", max(evs)), ("gc", ld_const_g)] + ([fdep] if fdep else []),
                                       inc=(k == NK - 1))
                        l2tb_free[li] = [("t", tl), ("sl%d" % li, stl)]
                        lg = gsc["lg"][:, :]
                        v0 = sc.op("v", lambda e, o=lg, i=psx[:, :NE]: e.tensor_copy(o, i),
                                   deps=[("t", tl), ("v", sc.cnt["v"]), ("s", sc.cnt["s"])])
                        fbank.consumed(j, "v", v0)
                        v1 = sc.op("v", lambda e, o=gsc["nm1"][:, :], i=lg:
                                   e.reduce_max(o, i, axis=AX.X, negate=True), deps=[("v", v0)])
                        v2 = sc.op("v", lambda e, o=gsc["im"][:, :], i=lg, m=gsc["nm1"][:, :]:
                                   e.tensor_scalar(o, i, m, 0.0, ALU.add, ALU.is_ge), deps=[("v", v1)])
                        v3 = sc.op("v", lambda e, o=gsc["mk"][:, :], i0=gsc["im"][:, :], i1=lg:
                                   e.scalar_tensor_tensor(o, i0, NEG, i1, ALU.mult, ALU.add),
                                   deps=[("v", v2)])
                        v4_ = sc.op("v", lambda e, o=gsc["nm2"][:, :], i=gsc["mk"][:, :]:
                                    e.reduce_max(o, i, axis=AX.X, negate=True), deps=[("v", v3)])
                        s1 = sc.op("s", lambda e, o=gsc["e1"][:, :], i=lg, b=gsc["nm1"][:, :]:
                                   e.activation(o, i, AF.Exp, bias=b), deps=[("v", v1), ("v", v3)])
                        s2 = sc.op("s", lambda e, o=gsc["den"][:, :], i=gsc["nm2"][:, :], b=gsc["nm1"][:, :]:
                                   e.activation(o, i, AF.Exp, bias=b, scale=-1.0), deps=[("v", v4_), ("s", s1)])
                        v5 = sc.op("v", lambda e, o=gsc["den"][:, :], i=gsc["den"][:, :]:
                                   e.tensor_scalar(o, i, 1.0, None, ALU.add), deps=[("s", s2)])
                        v6 = sc.op("v", lambda e, o=gsc["rcd"][:, :], i=gsc["den"][:, :]:
                                   e.reciprocal(o, i), deps=[("v", v5)])
                        v7 = sc.op("v", lambda e, o=gsc["mk"][:, :], i=lg, m=gsc["nm2"][:, :]:
                                   e.tensor_scalar(o, i, m, 0.0, ALU.add, ALU.is_ge), deps=[("v", v4_), ("s", s2)])
                        v8 = sc.op("v", lambda e, o=gsc["e1"][:, :], i=gsc["e1"][:, :], m=gsc["rcd"][:, :]:
                                   e.tensor_scalar(o, i, m, None, ALU.mult), deps=[("v", v6), ("s", s1)])
                        v9 = sc.op("v", lambda e, o=gates[:, gt_g * NE:(gt_g + 1) * NE],
                                   i0=gsc["e1"][:, :], i1=gsc["mk"][:, :]:
                                   e.tensor_mul(o, i0, i1), deps=[("v", v7), ("v", v8)])
                        sg = sc.op("s", lambda e, o=gbf[:, :], i=gates[:, gt_g * NE:(gt_g + 1) * NE]:
                                   e.copy(o, i), deps=[("v", v9)])
                        jb, pbx, fdepb = bbank.acquire()
                        tg = sc.op("t", lambda e, o=pbx[:NE, 0, :], i=gbf[:, :], idq=idm_b[:, :]:
                                   e.transpose(o, i, idq), deps=[("s", sg)] + ([fdepb] if fdepb else []))
                        sgt = sc.op("s", lambda e, o=gatesT[:, gt_g * 128:(gt_g + 1) * 128], i=pbx[:NE, 0, :]:
                                    e.copy(o, i), deps=[("t", tg)])
                        bbank.consumed(jb, "s", sgt)

                    for tt in range(8):
                        bi = tt % 3
                        rl = sp * 1024 + tt * 128
                        ldx1 = sc.dma("y", "x1t%d" % bi, lambda e, o=x1t[bi][:, :], i=x1s[rl:rl + 128, :]:
                                      e.dma_start(o, i),
                                      deps=[("sx0", sc.dslot.get("sx0", 0)),
                                            ("sx1", sc.dslot.get("sx1", 0))] + x1t_free[bi])
                        st = stB_war[bi]
                        for nm in statsB:
                            st[nm] = statsB[nm][bi][:, :]
                        st["sq"] = sqsB[:, :]
                        _ln_a(sc, x1t[bi][:, :], st, deps=[("x1t%d" % bi, ldx1)])
                        if tt >= 1:
                            p2_finish(tt - 1)
                    p2_finish(7)
                sc.barrier()

        if debug:
            sc.dma("s", "so", lambda e: e.dma_start(dbg_g[:, :], gates[:, :]))
        sc.flush()


        from concourse.bass_core_rust import IndirectOffsetOnAxis

        abw.close()

        with ExitStack() as c0:
            a = c0.enter_context
            selT = a(nc.sbuf_tensor("selT", [NE, TOK], F32))
            pfx = [a(nc.sbuf_tensor("pfx%d" % i, [NE, TOK], F32)) for i in range(2)]
            mp_tok = a(nc.sbuf_tensor("mp_tok", [128, 16, NE], F32))
            Bt = a(nc.sbuf_tensor("Bt", [128, 16, CAP], F32))
            comb = a(nc.sbuf_tensor("comb", [128, 16, 3], F32))
            idxg = a(nc.sbuf_tensor("idxg", [4, CAP], F32))
            iop1_b = a(nc.sbuf_tensor("iop1_b", [128, CAP], F32))
            tokv_b = a(nc.sbuf_tensor("tokv_b", [128, 16], F32))

            BIGC = 65536.0
            sc.dma("y", "yc2", lambda e: e.dma_start(iop1_b[:, :], iotap1))
            ldc2 = sc.dma("y", "yc2", lambda e: e.dma_start(tokv_b[:, :], tokv))
            v0 = sc.op("v", lambda e: e.tensor_scalar(selT[:, :], gatesT[:, :], 0.0,
                                                      None, ALU.is_gt),
                       deps=[("s", sc.cnt["s"])])
            vp = sc.op("v", lambda e: e.tensor_copy(pfx[0][:, :], selT[:, :]),
                       deps=[("v", v0)])
            cur = 0
            sh = 1
            while sh < TOK:
                nxt = 1 - cur
                vp1 = sc.op("v", lambda e, c=cur, n=nxt, s_=sh:
                            e.tensor_add(pfx[n][:, s_:], pfx[c][:, s_:], pfx[c][:, :TOK - s_]),
                            deps=[("v", vp)])
                vp = sc.op("v", lambda e, c=cur, n=nxt, s_=sh:
                           e.tensor_copy(pfx[n][:, :s_], pfx[c][:, :s_]),
                           deps=[("v", vp1)])
                cur = nxt
                sh *= 2
            vb = sc.op("v", lambda e, c=cur: e.tensor_scalar(pfx[1 - c][:, :], pfx[c][:, :],
                                                             BIGC, None, ALU.add),
                       deps=[("v", vp)])
            vm_ = sc.op("v", lambda e, c=cur: e.scalar_tensor_tensor(
                pfx[c][:, :], selT[:, :], -BIGC, pfx[1 - c][:, :], ALU.mult, ALU.add),
                deps=[("v", vb)])
            j, psx, fdep = fbank.acquire()
            tl = None
            for g in range(16):
                tl = sc.op("t", lambda e, o=psx[:, g * 8:(g + 1) * 8],
                           i=pfx[cur][:, g * 128:(g + 1) * 128], idq=idmf_b[:NE, :NE]:
                           e.transpose(o, i, idq),
                           deps=[("v", vm_)] + ([fdep] if fdep and g == 0 else []),
                           inc=(g == 15))
            sev = sc.op("s", lambda e, i=psx[:, :128].rearrange("p (g e) -> p g e", e=NE):
                        e.copy(mp_tok[:, :, :], i), deps=[("t", tl)])
            fbank.consumed(j, "s", sev)
            ite = []
            for ei in range(NE):
                bts = []
                for g in range(16):
                    bts.append(sc.op("v", lambda e, g=g, ei=ei:
                                     e.tensor_scalar(Bt[:, g, :], iop1_b[:, :],
                                                     mp_tok[:, g, ei:ei + 1], None, ALU.is_equal),
                                     deps=[("s", sev), ("yc2", ldc2),
                                           ("t", sc.cnt["t"]), ("s", sc.cnt["s"])]))
                vcb = sc.op("v", lambda e, ei=ei: e.tensor_copy(comb[:, :, 1:2],
                                                                gates[:, ei::NE].rearrange("p (g o) -> p g o", o=1)),
                            deps=[("v", sc.cnt["v"]), ("t", sc.cnt["t"])])
                vcb2 = sc.op("v", lambda e: e.tensor_copy(comb[:, :, 0:1],
                                                          tokv_b[:, :].rearrange("p (g o) -> p g o", o=1)),
                             deps=[("yc2", ldc2), ("t", sc.cnt["t"])])
                vcb3 = sc.op("v", lambda e: e.memset(comb[:, :, 2:3], 1.0),
                             deps=[("t", sc.cnt["t"])])
                evs2 = []
                for (cs0, csw) in ((0, 512), (512, 128)):
                    j, psx, fdep = fbank.acquire()
                    tl = None
                    for g in range(16):
                        tl = sc.op("t", lambda e, g=g, cs0=cs0, csw=csw,
                                   o=psx[:3, :csw]:
                                   e.matmul(o, comb[:, g, :], Bt[:, g, cs0:cs0 + csw],
                                            start=(g == 0), stop=(g == 15)),
                                   deps=[("v", max(bts)), ("v", vcb), ("v", vcb2), ("v", vcb3)] +
                                        ([fdep] if fdep and g == 0 else []),
                                   inc=(g == 15))
                    ev2 = sc.op("s", lambda e, o=idxg[:3, cs0:cs0 + csw], i=psx[:3, :csw]:
                                e.copy(o, i), deps=[("t", tl)])
                    fbank.consumed(j, "s", ev2)
                    evs2.append(ev2)
                sev2 = max(evs2)
                j2, ps2, fdep2 = fbank.acquire()
                tl2 = None
                for t5 in range(5):
                    tl2 = sc.op("t", lambda e, t5=t5, o=ps2[:, t5 * 4:t5 * 4 + 3]:
                                e.transpose(o, idxg[:3, t5 * 128:(t5 + 1) * 128], idmf_b[:3, :3]),
                                deps=[("s", sev2)] + ([fdep2] if fdep2 and t5 == 0 else []),
                                inc=(t5 == 4))
                sev3 = sc.op("s", lambda e, o=idxT[:, ei * 5:(ei + 1) * 5, :],
                             i=ps2[:, :20].rearrange("p (t c) -> p t c", c=4):
                             e.copy(o, i), deps=[("t", tl2)])
                fbank.consumed(j2, "s", sev3)
                ite.append(sev3)
            vii = sc.op("v", lambda e: e.tensor_copy(idx_i[:, :], idxT[:, :, 0]),
                        deps=[("s", max(ite))])
            idx_ready = [("v", vii), ("s", max(ite))]

        sc.barrier()

        gsb_free = [[], []]
        lrg_free = [[], []]
        bounce_done = {}

        def emit_bounce(ei):
            par = ei % 2
            stores = []
            for t5 in range(5):
                gb = t5 % 2
                gx = sc.dma("g", "gx%d" % gb, lambda e, t5=t5, ei=ei, o=g_sb[gb][:, :]:
                            e.indirect_dma_start(
                                o, None, l2r,
                                IndirectOffsetOnAxis(idx_i[:, ei * 5 + t5:ei * 5 + t5 + 1], 0)),
                            deps=idx_ready + sr_full + gsb_free[gb])
                gr = sc.dma("s", "gr%d%d" % (par, gb), lambda e, t5=t5, par=par, gb=gb,
                            o=l2rg[par, t5 * 128:(t5 + 1) * 128, :], i=g_sb[gb][:, :]:
                            e.dma_start(o, i), deps=[("gx%d" % gb, gx)] + lrg_free[par])
                gsb_free[gb] = [("gr%d%d" % (par, gb), gr)]
                stores.append(("gr%d%d" % (par, gb), gr))
            bounce_done[ei] = stores[-2:]

        l2t_free = []
        with ExitStack() as cs:
            a = cs.enter_context
            l2t = a(nc.sbuf_tensor("l2t", [128, NK, 1024], BF16))
            w1b = [a(nc.sbuf_tensor("w1b%d" % i, [128, NK, 512], BF16)) for i in range(2)]
            w2b = [a(nc.sbuf_tensor("w2b%d" % i, [128, 8, 512], BF16)) for i in range(2)]
            hraw = a(nc.sbuf_tensor("hraw", [128, 8, 1024], BF16))
            acc = a(nc.sbuf_tensor("acc", [128, 8, 2048], F32))
            x1cb = [a(nc.sbuf_tensor("x1cb%d" % i, [128, 512], F32)) for i in range(2)]

            w1_free = [[], []]
            w2_free = [[], []]
            x1cb_free = [[], []]
            w1_i = 0
            w2_i = 0
            sr_full = [("sr0", sc.dslot.get("sr0", 0)), ("sr1", sc.dslot.get("sr1", 0))]
            vms0 = sc.op("v", lambda e: e.memset(g_sb[0][:, :], 0.0), deps=[])
            vms1 = sc.op("v", lambda e: e.memset(g_sb[1][:, :], 0.0), deps=[])
            gsb_free[0] = [("v", vms0)]
            gsb_free[1] = [("v", vms1)]
            emit_bounce(0)
            emit_bounce(1)

            for s in range(2):
                tok0 = s * 1024
                ldl2 = sc.dma("y", "l2t", lambda e, o=l2t[:, :, :],
                              i=l2s.rearrange("(k p) t -> p k t", p=128)[:, :, tok0:tok0 + 1024]:
                              e.dma_start(o, i),
                              deps=[("sl0", sc.dslot.get("sl0", 0)),
                                    ("sl1", sc.dslot.get("sl1", 0))] + l2t_free)
                acc_v = [[0] * 4 for _ in range(8)]
                so_before = sc.dslot.get("so", 0)
                for ei in (8, 9):
                    half = ei - NE
                    w1blocks = [sw1[:, half * 1024:half * 1024 + 512],
                                sw1[:, half * 1024 + 512:half * 1024 + 1024]]
                    w2q = lambda dc, half=half: sw2[half * 1024:(half + 1) * 1024,
                                                    dc * 512:(dc + 1) * 512]
                    bcol = lambda hc, half=half: sb1t_b[:, half * 8 + hc:half * 8 + hc + 1]
                    hdone = []
                    for blk in range(2):
                        wi = w1_i % 2
                        w1_i += 1
                        ldw1 = sc.dma("g", "w1%d" % wi, lambda e, o=w1b[wi][:, :, :],
                                      i=w1blocks[blk].rearrange("(k p) h -> p k h", p=128):
                                      e.dma_start(o, i), deps=w1_free[wi])
                        for hc2 in range(4):
                            hcf = blk * 4 + hc2
                            for tc in range(2):
                                j, psx, fdep = fbank.acquire()
                                mmdeps = [("w1%d" % wi, ldw1), ("l2t", ldl2)] + \
                                         ([fdep] if fdep else [])
                                tl = None
                                for k in range(NK):
                                    tl = sc.op("t", lambda e, o=psx[:, :], l=w1b[wi][:, k, hc2 * 128:(hc2 + 1) * 128],
                                               r=l2t[:, k, tc * 512:(tc + 1) * 512],
                                               st=(k == 0), sp_=(k == NK - 1):
                                               e.matmul(o, l, r, start=st, stop=sp_),
                                               deps=mmdeps, inc=(k == NK - 1))
                                sh = sc.op("s", lambda e, o=hraw[:, hcf, tc * 512:(tc + 1) * 512],
                                           i=psx[:, :], b=bcol(hcf):
                                           e.activation(o, i, AF.Gelu, bias=b), deps=[("t", tl)])
                                fbank.consumed(j, "s", sh)
                                hdone.append(sh)
                        w1_free[wi] = [("t", sc.cnt["t"])]
                    h_ready = [("s", max(hdone))]
                    for dc in range(4):
                        wi = w2_i % 2
                        w2_i += 1
                        ldw2 = sc.dma("g", "w2%d" % wi, lambda e, o=w2b[wi][:, :, :],
                                      i=w2q(dc).rearrange("(k p) d -> p k d", p=128):
                                      e.dma_start(o, i), deps=w2_free[wi])
                        for tt in range(8):
                            j, psx, fdep = fbank.acquire()
                            mmdeps = h_ready + [("w2%d" % wi, ldw2), ("gc", ld_const_g),
                                                ("yc", ld_const_y)] + ([fdep] if fdep else [])
                            tl = None
                            for k in range(8):
                                last = (k == 7) and ei != 9
                                tl = sc.op("t", lambda e, o=psx[:, :], l=hraw[:, k, tt * 128:(tt + 1) * 128],
                                           r=w2b[wi][:, k, :], st=(k == 0), sp_=last:
                                           e.matmul(o, l, r, start=st, stop=sp_),
                                           deps=mmdeps, inc=last)
                            if ei == 9:
                                gt_g = s * 8 + tt
                                sc.op("t", lambda e, o=psx[:, :], l=gatesT[:, gt_g * 128:(gt_g + 1) * 128],
                                      r=b2_b[:, dc * 512:(dc + 1) * 512]:
                                      e.matmul(o, l, r, start=False, stop=False),
                                      deps=mmdeps, inc=False)
                                tl = sc.op("t", lambda e, o=psx[:, :], l=ones_b[:, :],
                                           r=sb2_b[:, dc * 512:(dc + 1) * 512]:
                                           e.matmul(o, l, r, start=False, stop=True),
                                           deps=mmdeps)
                            rl = tok0 + tt * 128
                            if ei == 8:
                                bi = tt % 2
                                ldxc = sc.dma("y", "xb%d" % bi, lambda e, o=x1cb[bi][:, :],
                                              i=x1s[rl:rl + 128, dc * 512:(dc + 1) * 512]:
                                              e.dma_start(o, i), deps=x1cb_free[bi])
                                in1 = x1cb[bi][:, :]
                                vdeps = [("t", tl), ("xb%d" % bi, ldxc)]
                                if so_before:
                                    vdeps.append(("so", so_before))
                            else:
                                in1 = acc[:, tt, dc * 512:(dc + 1) * 512]
                                vdeps = [("t", tl), ("v", acc_v[tt][dc])]
                            va = sc.op("v", lambda e, o=acc[:, tt, dc * 512:(dc + 1) * 512],
                                       i0=psx[:, :], i1=in1:
                                       e.scalar_tensor_tensor(o, i0, 1.0, i1, ALU.mult, ALU.add),
                                       deps=vdeps)
                            fbank.consumed(j, "v", va)
                            acc_v[tt][dc] = va
                            if ei == 8:
                                x1cb_free[bi] = [("v", va)]
                            if ei == 9:
                                sc.dma("s", "so",
                                       lambda e, o=out[rl:rl + 128, dc * 512:(dc + 1) * 512],
                                       i=acc[:, tt, dc * 512:(dc + 1) * 512]:
                                       e.dma_start(o, i), deps=[("v", va)])
                        w2_free[wi] = [("t", sc.cnt["t"])]
                l2t_free = [("t", sc.cnt["t"])]

        sc.barrier()

        with ExitStack() as c2:
            a = c2.enter_context
            l2gT = [a(nc.sbuf_tensor("l2gT%d" % i, [128, NK, CAP], BF16)) for i in range(2)]
            hrawE = a(nc.sbuf_tensor("hrawE", [128, 8, CAP], BF16))
            out_sb = [a(nc.sbuf_tensor("out_sb%d" % i, [128, 5, 2048], BF16)) for i in range(2)]
            w1b2 = [a(nc.sbuf_tensor("w1c%d" % i, [128, NK, 512], BF16)) for i in range(2)]
            w2b2 = [a(nc.sbuf_tensor("w2c%d" % i, [128, 8, 512], BF16)) for i in range(2)]

            ltt_free = [[], []]
            w1_free2 = [[], []]
            w2_free2 = [[], []]
            out_free = [[[] for _ in range(5)] for _ in range(2)]
            w1_j = 0
            w2_j = 0

            so_full = ("so", sc.dslot.get("so", 0))

            def emit_lt(ei):
                par = ei % 2
                stores = bounce_done[ei]
                lds = []
                for k in range(NK):
                    lt = sc.dma("y", "lt%d" % par, lambda e, k=k, par=par,
                                o=l2gT[par][:, k, :], i=l2rg[par, :, k * 128:(k + 1) * 128]:
                                e.dma_start_transpose(o, i),
                                deps=stores + ltt_free[par])
                    lds.append(lt)
                lrg_free[par] = [("lt%d" % par, lds[-1])]
                return ("lt%d" % par, lds[-1])

            def emit_gather(ei):
                emit_bounce(ei)
                return emit_lt(ei)

            pend_ld = {}
            pend_ld[0] = emit_lt(0)
            pend_ld[1] = emit_lt(1)
            pending_scatter = None
            for ei in range(NE):
                par = ei % 2
                ldlast = pend_ld[ei]
                wi0 = w1_j % 2
                ldw1_0 = sc.dma("y", "wc1%d" % wi0, lambda e, wi0=wi0, ei=ei, o=w1b2[wi0][:, :, :],
                                i=w1s[ei, :, 0:512].rearrange("(k p) h -> p k h", p=128):
                                e.dma_start(o, i),
                                deps=w1_free2[wi0] + [("wcv%d" % ei, 2)])
                if pending_scatter is not None:
                    pending_scatter()
                    pending_scatter = None
                hdone = []
                for blk in range(2):
                    wi = w1_j % 2
                    w1_j += 1
                    if blk == 0:
                        ldw1 = ldw1_0
                    else:
                        ldw1 = sc.dma("y", "wc1%d" % wi, lambda e, wi=wi, ei=ei, blk=blk, o=w1b2[wi][:, :, :],
                                      i=w1s[ei, :, blk * 512:(blk + 1) * 512].rearrange("(k p) h -> p k h", p=128):
                                      e.dma_start(o, i),
                                      deps=w1_free2[wi] + [("wcv%d" % ei, 2)])
                    for hc2 in range(4):
                        hcf = blk * 4 + hc2
                        for (cs0, csw) in ((0, 512), (512, 128)):
                            j, psx, fdep = fbank.acquire()
                            mmdeps = [("wc1%d" % wi, ldw1), ldlast] + ([fdep] if fdep else [])
                            tl = None
                            for k in range(NK):
                                tl = sc.op("t", lambda e, o=psx[:, :csw], l=w1b2[wi][:, k, hc2 * 128:(hc2 + 1) * 128],
                                           r=l2gT[par][:, k, cs0:cs0 + csw],
                                           st=(k == 0), sp_=(k == NK - 1):
                                           e.matmul(o, l, r, start=st, stop=sp_),
                                           deps=mmdeps, inc=(k == NK - 1))
                            sh = sc.op("s", lambda e, o=hrawE[:, hcf, cs0:cs0 + csw],
                                       i=psx[:, :csw], b=b1t_b[:, ei * 8 + hcf:ei * 8 + hcf + 1]:
                                       e.activation(o, i, AF.Gelu, bias=b), deps=[("t", tl)])
                            fbank.consumed(j, "s", sh)
                            hdone.append(sh)
                    w1_free2[wi] = [("t", sc.cnt["t"])]
                h_ready = [("s", max(hdone))]
                ltt_free[par] = [("t", sc.cnt["t"])]
                tsv = [[0] * 4 for _ in range(5)]
                for dc in range(4):
                    wi = w2_j % 2
                    w2_j += 1
                    ldw2 = sc.dma("y", "wc2%d" % wi, lambda e, o=w2b2[wi][:, :, :],
                                  i=w2s[ei, :, dc * 512:(dc + 1) * 512].rearrange("(k p) d -> p k d", p=128):
                                  e.dma_start(o, i),
                                  deps=w2_free2[wi] + [("wcv%d" % ei, 2)])
                    for t5 in range(5):
                        j, psx, fdep = fbank.acquire()
                        mmdeps = h_ready + [("wc2%d" % wi, ldw2)] + ([fdep] if fdep else [])
                        tl = None
                        for k in range(8):
                            tl = sc.op("t", lambda e, o=psx[:, :], l=hrawE[:, k, t5 * 128:(t5 + 1) * 128],
                                       r=w2b2[wi][:, k, :], st=(k == 0), sp_=(k == 7):
                                       e.matmul(o, l, r, start=st, stop=sp_),
                                       deps=mmdeps, inc=(k == 7))
                        vts = sc.op("v", lambda e, t5=t5, dc=dc,
                                    o=out_sb[par][:, t5, dc * 512:(dc + 1) * 512],
                                    i=psx[:, :], g=idxT[:, ei * 5 + t5, 1:2]:
                                    e.tensor_scalar(o, i, g, None, ALU.mult),
                                    deps=[("t", tl)] + (out_free[par][t5] if dc == 0 else []))
                        fbank.consumed(j, "v", vts)
                        tsv[t5][dc] = vts
                    w2_free2[wi] = [("t", sc.cnt["t"])]
                if ei + 2 < NE:
                    pend_ld[ei + 2] = emit_gather(ei + 2)

                def mk_scatter(ei=ei, tsv=tsv, par=par):
                    for t5 in range(5):
                        sca = sc.dma("g", "sca", lambda e, t5=t5, ei=ei, i=out_sb[par][:, t5, :]:
                                     e.indirect_dma_start(
                                         out, IndirectOffsetOnAxis(idx_i[:, ei * 5 + t5:ei * 5 + t5 + 1], 0),
                                         i, None, compute_op=ALU.add),
                                     deps=[("v", max(tsv[t5])), so_full,
                                           ("sca", sc.dslot.get("sca", 0))])
                        out_free[par][t5] = [("sca", sca)]
                pending_scatter = mk_scatter
            pending_scatter()

        sc.wait_only("y", [("so", sc.dslot.get("so", 0)),
                           ("sca", sc.dslot.get("sca", 0))])
        sc.wait_only("s", [("sca", sc.dslot.get("sca", 0))])
        sc.flush()

    return nc


def _host_inputs(inputs):
    x = np.ascontiguousarray(inputs["L0_attn_x"], dtype=np.float32)
    wqkv = np.asarray(inputs["L0_attn_W_qkv"], dtype=np.float32)
    shared = {
        "wq": np.ascontiguousarray(wqkv[:, 0::3]),
        "wk": np.ascontiguousarray(wqkv[:, 1::3]),
        "wv": np.ascontiguousarray(wqkv[:, 2::3]),
        "wo": np.ascontiguousarray(inputs["L0_attn_W_out"], dtype=np.float32),
        "bo": np.ascontiguousarray(inputs["L0_attn_b_out"], dtype=np.float32).reshape(1, D),
        "gw": np.ascontiguousarray(inputs["L0_ffn_gate_w"], dtype=np.float32),
        "b1t": np.ascontiguousarray(
            np.asarray(inputs["L0_ffn_b1"], dtype=np.float32).reshape(NE, 8, 128)
            .transpose(2, 0, 1).reshape(128, NE * 8)),
        "b2": np.ascontiguousarray(inputs["L0_ffn_b2"], dtype=np.float32),
        "w1": np.ascontiguousarray(inputs["L0_ffn_w1"], dtype=np.float32),
        "w2": np.ascontiguousarray(inputs["L0_ffn_w2"], dtype=np.float32),
        "sw1": np.ascontiguousarray(inputs["L0_ffn_shared_w1"], dtype=np.float32),
        "sb1t": np.ascontiguousarray(
            np.asarray(inputs["L0_ffn_shared_b1"], dtype=np.float32).reshape(16, 128).T),
        "sw2": np.ascontiguousarray(inputs["L0_ffn_shared_w2"], dtype=np.float32),
        "sb2": np.ascontiguousarray(
            inputs["L0_ffn_shared_b2"], dtype=np.float32).reshape(1, D),
        "rt": _rot_matrix(),
        "iotap1": np.broadcast_to(np.arange(1, CAP + 1, dtype=np.float32)[None, :],
                                  (128, CAP)).copy(),
        "tokv": np.ascontiguousarray(
            (np.arange(16, dtype=np.float32)[None, :] * 128
             + np.arange(128, dtype=np.float32)[:, None])),
        "idm": np.eye(128, dtype=ml_dtypes.bfloat16),
        "idmf": np.eye(128, dtype=np.float32),
        "onesb": np.ones((1, 128), dtype=ml_dtypes.bfloat16),
        "maskB": _mask_b(),
    }
    cores = []
    for c in range(8):
        b, r0 = c // 4, (c % 4) * TOK
        halo = (np.zeros((HALO, D), np.float32) if r0 == 0
                else x[b, r0 - HALO:r0])
        xh = np.ascontiguousarray(np.concatenate([halo, x[b, r0:r0 + TOK]], axis=0))
        pos = np.clip(np.arange(r0 - HALO, r0 + TOK), 0, None).astype(np.float64)
        inv = 1.0 / (10000.0 ** (np.arange(0, HD, 2, dtype=np.float64) / HD))
        ang = np.repeat(pos[:, None] * inv[None, :], 2, axis=1)
        core = dict(shared)
        core["xh"] = xh
        core["cost"] = np.ascontiguousarray(np.cos(ang).T.astype(np.float32))
        core["sint"] = np.ascontiguousarray(np.sin(ang).T.astype(np.float32))
        core["maskA"] = _mask_a() if r0 == 0 else shared["maskB"]
        cores.append(core)
    return cores


def _rot_matrix():
    r = np.zeros((128, 128), np.float32)
    for k in range(64):
        r[2 * k, 2 * k + 1] = -1.0
        r[2 * k + 1, 2 * k] = 1.0
    return r.T.astype(ml_dtypes.bfloat16)


def _mask_b():
    m = np.full((128, 256), NEG, np.float32)
    r = np.arange(128)
    m[:, :128][r[:, None] <= np.arange(128)[None, :]] = 0.0
    m[:, 128:][r[:, None] >= np.arange(128)[None, :]] = 0.0
    return m


def _mask_a():
    m = _mask_b().copy()
    m[:, :128] = NEG
    return m


def kernel(**inputs):
    cores = _host_inputs(inputs)
    nc = build_program()
    res = run_bass_kernel_spmd(nc, cores, core_ids=list(range(8)))
    outp = np.empty((2, 8192, D), np.float32)
    for c in range(8):
        b, r0 = c // 4, (c % 4) * TOK
        outp[b, r0:r0 + TOK] = res.results[c]["out"]
    return outp


# version 19
# speedup vs baseline: 1.0742x; 1.0656x over previous
import math
import numpy as np
import ml_dtypes
import concourse.bass as bass
import concourse.mybir as mybir
from concourse.bass_utils import run_bass_kernel_spmd

F32 = mybir.dt.float32
BF16 = mybir.dt.bfloat16
AF = mybir.ActivationFunctionType
ALU = mybir.AluOpType
AX = mybir.AxisListType

D = 2048
NH = 16
HD = 128
TOK = 2048
HALO = 128
NK = D // 128
SCALE = 1.0 / math.sqrt(HD)
NEG = -1.0e38
EPS = 1e-5
NE = 8
HID = 1024
SH = 2048
CAP = 640


class Sched:

    def __init__(self, nc, ctx):
        self.nc = nc
        self.ctx = ctx
        self.ops = {k: [] for k in ("t", "v", "s", "y", "g")}
        self.cnt = dict(t=0, v=0, s=0)
        self.dslot = {}
        self.last_wait = {q: {} for q in self.ops}
        self.sems = {}
        for ck in ("t", "v", "s"):
            self.sems[ck] = ctx.enter_context(nc.semaphore("sem_" + ck))

    def _dma_sem(self, slot):
        if slot not in self.sems:
            self.sems[slot] = self.ctx.enter_context(
                self.nc.semaphore("sem_" + slot))
            self.dslot[slot] = 0
        return self.sems[slot]

    def _waits(self, q, deps):
        ws = []
        for ck, val in deps:
            if val is None or val <= 0:
                continue
            if ck == "t" and q == "t":
                continue
            if val > self.last_wait[q].get(ck, 0):
                self.last_wait[q][ck] = val
                step = 1 if ck in ("t", "v", "s") else 16
                if ck not in self.sems:
                    self._dma_sem(ck)
                ws.append((self.sems[ck], val * step))
        return ws

    def op(self, q, fn, deps=(), inc=True):
        assert q in ("t", "v", "s")
        ws = self._waits(q, deps)
        sem = None
        if inc:
            self.cnt[q] += 1
            sem = self.sems[q]

        def emit(eng, ws=ws, fn=fn, sem=sem):
            for s_, v_ in ws:
                eng.wait_ge(s_, v_)
            inst = fn(eng)
            if sem is not None:
                inst.then_inc(sem, 1)

        self.ops[q].append(emit)
        return self.cnt[q]

    def dma(self, q, slot, fn, deps=()):
        sem = self._dma_sem(slot)
        ws = self._waits(q, deps)
        self.dslot[slot] += 1

        def emit(eng, ws=ws, fn=fn, sem=sem):
            for s_, v_ in ws:
                eng.wait_ge(s_, v_)
            fn(eng).then_inc(sem, 16)

        self.ops[q].append(emit)
        return self.dslot[slot]

    def wait_only(self, q, deps):
        ws = self._waits(q, deps)
        if not ws:
            return

        def emit(eng, ws=ws):
            for s_, v_ in ws:
                eng.wait_ge(s_, v_)

        self.ops[q].append(emit)

    def barrier(self):
        snap = dict(self.cnt)
        snap.update(self.dslot)
        for q in self.ops:
            deps = [(ck, v) for ck, v in snap.items() if v > 0 and ck != q]
            self.wait_only(q, deps)

    def flush(self):
        with self.nc.Block() as block:
            for q, eng_name in (("t", "tensor"), ("v", "vector"),
                                ("s", "scalar"), ("y", "sync"), ("g", "gpsimd")):
                ops = self.ops[q]
                if not ops:
                    continue
                getattr(block, eng_name)(lambda e, ops=ops: [f(e) for f in ops])
        self.ops = {k: [] for k in self.ops}


class Banks:

    def __init__(self, tensors):
        self.t = tensors
        self.free_dep = [None] * len(tensors)
        self.i = 0

    def acquire(self):
        j = self.i % len(self.t)
        self.i += 1
        return j, self.t[j], self.free_dep[j]

    def consumed(self, j, ck, count):
        self.free_dep[j] = (ck, count)


def _ln_a(sc, xt_ap, st, deps):
    v1 = sc.op("v", lambda e: e.reduce_sum(st["ssum"], xt_ap, axis=AX.X),
               deps=deps + st["war"])
    s1 = sc.op("s", lambda e: e.activation(st["sq"], xt_ap, AF.Square,
                                           accum_out=st["ssq"]),
               deps=deps + st["war"] + [("s", sc.cnt["s"])])
    st["a"] = (v1, s1)


def _ln_b(sc, xt_ap, st, out_bf_ap, out_deps=()):
    v1, s1 = st["a"]
    ssum, ssq, mean, m2, var, rstd = (st["ssum"], st["ssq"], st["mean"],
                                      st["m2"], st["var"], st["rstd"])
    v3 = sc.op("v", lambda e: e.tensor_scalar(mean, ssum, 1.0 / D, None, ALU.mult),
               deps=[("v", v1)])
    v5 = sc.op("v", lambda e: e.tensor_mul(m2, mean, mean), deps=[("v", v3)])
    v6 = sc.op("v", lambda e: e.scalar_tensor_tensor(var, ssq, 1.0 / D, m2,
                                                     ALU.mult, ALU.subtract),
               deps=[("s", s1), ("v", v5)])
    v7 = sc.op("v", lambda e: e.tensor_scalar(var, var, EPS, None, ALU.add),
               deps=[("v", v6)])
    s2 = sc.op("s", lambda e: e.activation(var, var, AF.Ln), deps=[("v", v7)])
    s3 = sc.op("s", lambda e: e.activation(rstd, var, AF.Exp, scale=-0.5),
               deps=[("s", s2)])
    s4 = sc.op("v", lambda e: e.tensor_scalar(out_bf_ap, xt_ap, mean, rstd,
                                              ALU.subtract, ALU.mult),
               deps=[("s", s3)] + list(out_deps))
    st["war"] = [("v", s4), ("s", s3)]
    return s4


def build_program(debug=False):
    nc = bass.Bass("TRN2", target_bir_lowering=False)

    xh = nc.dram_tensor("xh", [HALO + TOK, D], F32, kind="ExternalInput").ap()
    wq = nc.dram_tensor("wq", [D, D], F32, kind="ExternalInput").ap()
    wk = nc.dram_tensor("wk", [D, D], F32, kind="ExternalInput").ap()
    wv = nc.dram_tensor("wv", [D, D], F32, kind="ExternalInput").ap()
    wo = nc.dram_tensor("wo", [D, D], F32, kind="ExternalInput").ap()
    bo = nc.dram_tensor("bo", [1, D], F32, kind="ExternalInput").ap()
    cost = nc.dram_tensor("cost", [128, HALO + TOK], F32, kind="ExternalInput").ap()
    sint = nc.dram_tensor("sint", [128, HALO + TOK], F32, kind="ExternalInput").ap()
    rt = nc.dram_tensor("rt", [128, 128], BF16, kind="ExternalInput").ap()
    idm = nc.dram_tensor("idm", [128, 128], BF16, kind="ExternalInput").ap()
    idmf = nc.dram_tensor("idmf", [128, 128], F32, kind="ExternalInput").ap()
    onesb = nc.dram_tensor("onesb", [1, 128], BF16, kind="ExternalInput").ap()
    maskA = nc.dram_tensor("maskA", [128, 256], F32, kind="ExternalInput").ap()
    maskB = nc.dram_tensor("maskB", [128, 256], F32, kind="ExternalInput").ap()
    gw = nc.dram_tensor("gw", [D, NE], F32, kind="ExternalInput").ap()
    b1t = nc.dram_tensor("b1t", [128, NE * 8], F32, kind="ExternalInput").ap()
    b2 = nc.dram_tensor("b2", [NE, D], F32, kind="ExternalInput").ap()
    w1 = nc.dram_tensor("w1", [NE, D, HID], F32, kind="ExternalInput").ap()
    w2 = nc.dram_tensor("w2", [NE, HID, D], F32, kind="ExternalInput").ap()
    sw1 = nc.dram_tensor("sw1", [D, SH], F32, kind="ExternalInput").ap()
    sb1t = nc.dram_tensor("sb1t", [128, 16], F32, kind="ExternalInput").ap()
    sw2 = nc.dram_tensor("sw2", [SH, D], F32, kind="ExternalInput").ap()
    sb2 = nc.dram_tensor("sb2", [1, D], F32, kind="ExternalInput").ap()
    iotap1 = nc.dram_tensor("iotap1", [128, CAP], F32, kind="ExternalInput").ap()
    tokv = nc.dram_tensor("tokv", [128, 16], F32, kind="ExternalInput").ap()

    x1s = nc.dram_tensor("x1s", [TOK, D], F32, kind="Internal").ap()
    l2s = nc.dram_tensor("l2s", [D, TOK], BF16, kind="Internal").ap()
    l2r = nc.dram_tensor("l2r", [TOK, D], BF16, kind="Internal").ap()
    l2rg = nc.dram_tensor("l2rg", [2, CAP, D], BF16, kind="Internal").ap()
    w1s = nc.dram_tensor("w1s", [NE, D, HID], BF16, kind="Internal").ap()
    w2s = nc.dram_tensor("w2s", [NE, HID, D], BF16, kind="Internal").ap()
    out = nc.dram_tensor("out", [TOK, D], F32, kind="ExternalOutput").ap()
    if debug:
        dbg_x1 = nc.dram_tensor("dbg_x1", [TOK, D], F32, kind="ExternalOutput").ap()
        dbg_l2 = nc.dram_tensor("dbg_l2", [D, TOK], BF16, kind="ExternalOutput").ap()
        dbg_g = nc.dram_tensor("dbg_g", [128, 16 * NE], F32, kind="ExternalOutput").ap()

    from contextlib import ExitStack
    with ExitStack() as ctx:
        sc = Sched(nc, ctx)

        ob = ctx.enter_context
        rt_b = ob(nc.sbuf_tensor("rt_b", [128, 128], BF16))
        idm_b = ob(nc.sbuf_tensor("idm_b", [128, 128], BF16))
        idmf_b = ob(nc.sbuf_tensor("idmf_b", [128, 128], F32))
        ones_b = ob(nc.sbuf_tensor("ones_b", [1, 128], BF16))
        mA_b = ob(nc.sbuf_tensor("mA_b", [128, 256], F32))
        mB_b = ob(nc.sbuf_tensor("mB_b", [128, 256], F32))
        bo_b = ob(nc.sbuf_tensor("bo_b", [1, D], BF16))
        gw_b = ob(nc.sbuf_tensor("gw_b", [128, NK, NE], BF16))
        b1t_b = ob(nc.sbuf_tensor("b1t_b", [128, NE * 8], F32))
        sb1t_b = ob(nc.sbuf_tensor("sb1t_b", [128, 16], F32))
        b2_b = ob(nc.sbuf_tensor("b2_b", [NE, D], BF16))
        sb2_b = ob(nc.sbuf_tensor("sb2_b", [1, D], BF16))
        gates = ob(nc.sbuf_tensor("gates", [128, 16 * NE], F32))
        gatesT = ob(nc.sbuf_tensor("gatesT", [NE, 16 * 128], BF16))
        idxT = ob(nc.sbuf_tensor("idxT", [128, NE * 5, 4], F32))
        idx_i = ob(nc.sbuf_tensor("idx_i", [128, NE * 5], mybir.dt.int32))


        g_sb = [ob(nc.sbuf_tensor("g_sb%d" % i, [128, D], BF16)) for i in range(2)]
        pf = [ob(nc.psum_tensor("pf%d" % i, [128, 512], F32)) for i in range(6)]
        pb = [ob(nc.psum_tensor("pb%d" % i, [128, 8, 128], BF16)) for i in range(2)]
        fbank = Banks(pf)
        bbank = Banks(pb)

        sc.dma("y", "yc", lambda e: e.dma_start(rt_b[:, :], rt))
        sc.dma("y", "yc", lambda e: e.dma_start(idm_b[:, :], idm))
        sc.dma("y", "yc", lambda e: e.dma_start(idmf_b[:, :], idmf))
        sc.dma("y", "yc", lambda e: e.dma_start(ones_b[:, :], onesb))
        sc.dma("y", "yc", lambda e: e.dma_start(mA_b[:, :], maskA))
        sc.dma("y", "yc", lambda e: e.dma_start(mB_b[:, :], maskB))
        sc.dma("y", "yc", lambda e: e.dma_start(b1t_b[:, :], b1t))
        ld_const_y = sc.dma("y", "yc", lambda e: e.dma_start(sb1t_b[:, :], sb1t))
        sc.dma("g", "gc", lambda e: e.dma_start(bo_b[:, :], bo))
        sc.dma("g", "gc", lambda e: e.dma_start(
            gw_b[:, :, :], gw.rearrange("(k p) e -> p k e", p=128)))
        sc.dma("g", "gc", lambda e: e.dma_start(b2_b[:, :], b2))
        ld_const_g = sc.dma("g", "gc", lambda e: e.dma_start(sb2_b[:, :], sb2))

        x1_store = {}
        l2_store_last = 0

        conv_jobs = []
        for cei in range(NE):
            conv_jobs.append((w1s[cei, :, :], w1[cei, :, :], cei))
            conv_jobs.append((w2s[cei, :, :], w2[cei, :, :], cei))
        conv_state = {"i": 0}

        def emit_conv(n=1):
            for _ in range(n):
                if conv_state["i"] >= len(conv_jobs):
                    return
                dstap, srcap, cei = conv_jobs[conv_state["i"]]
                conv_state["i"] += 1
                sc.dma("g", "wcv%d" % cei, lambda e, o=dstap, i=srcap:
                       e.dma_start(o, i), deps=[])

        abw = ExitStack()
        cos_b = abw.enter_context(nc.sbuf_tensor("cos_b", [128, HALO + TOK], BF16))
        sin_b = abw.enter_context(nc.sbuf_tensor("sin_b", [128, HALO + TOK], BF16))
        sc.dma("g", "gc", lambda e: e.dma_start(cos_b[:, :], cost))
        ld_const_g = sc.dma("g", "gc", lambda e: e.dma_start(sin_b[:, :], sint))
        x1_store = {}
        for sp in range(2):
            from contextlib import ExitStack as _ES
            with _ES() as spsc:
                a = spsc.enter_context
                lnxT = a(nc.sbuf_tensor("lnxT%d" % sp, [128, NK, HALO + 1024], BF16))
                attnT = a(nc.sbuf_tensor("attnT%d" % sp, [128, NH, 1024], BF16))
                base = sp * 1024

                with _ES() as asc:
                    aa = asc.enter_context
                    xt = [aa(nc.sbuf_tensor("xt%d_%d" % (sp, i), [128, D], F32)) for i in range(3)]
                    sqs = aa(nc.sbuf_tensor("sqs%d" % sp, [128, D], F32))
                    lnb_bufs = [aa(nc.sbuf_tensor("lnb%d_%d" % (sp, i), [128, D], BF16)) for i in range(2)]
                    stats = {}
                    for nm in ("ssum", "ssq", "mean", "m2", "var", "rstd"):
                        stats[nm] = [aa(nc.sbuf_tensor("stA%d_%s%d" % (sp, nm, i), [128, 1], F32))
                                     for i in range(3)]
                    xt_free = [[] for _ in range(3)]
                    lnb_free = [[] for _ in range(2)]
                    st_war = [{"war": []} for _ in range(3)]
                    lnxT_evac = []

                    def ln_finish(tt):
                        bi = tt % 3
                        li = tt % 2
                        lnb = lnb_bufs[li]
                        st = st_war[bi]
                        s4 = _ln_b(sc, xt[bi][:, :], st, lnb[:, :], out_deps=lnb_free[li])
                        xt_free[bi] = [("v", s4)]
                        evs = []
                        for g2 in range(2):
                            j, pbx, fdep = bbank.acquire()
                            tdeps = [("v", s4), ("yc", ld_const_y)] + ([fdep] if fdep else [])
                            tl = None
                            for k2 in range(8):
                                k = g2 * 8 + k2
                                tl = sc.op("t", lambda e, o=pbx[:, k2, :],
                                           i=lnb[:, k * 128:(k + 1) * 128], idq=idm_b[:, :]:
                                           e.transpose(o, i, idq),
                                           deps=tdeps if k2 == 0 else [("v", s4)],
                                           inc=(k2 == 7))
                            ev = sc.op("s", lambda e, o=lnxT[:, g2 * 8:(g2 + 1) * 8, tt * 128:(tt + 1) * 128],
                                       i=pbx[:, :, :]: e.copy(o, i), deps=[("t", tl)])
                            bbank.consumed(j, "s", ev)
                            evs.append(ev)
                        lnb_free[li] = [("t", sc.cnt["t"])]
                        lnxT_evac.append(evs[-1])

                    for tt in range(9):
                        bi = tt % 3
                        r0 = base + tt * 128
                        ldx = sc.dma("y", "xt%d" % bi, lambda e, o=xt[bi][:, :], i=xh[r0:r0 + 128, :]:
                                     e.dma_start(o, i), deps=xt_free[bi])
                        st = st_war[bi]
                        for nm in stats:
                            st[nm] = stats[nm][bi][:, :]
                        st["sq"] = sqs[:, :]
                        _ln_a(sc, xt[bi][:, :], st, deps=[("xt%d" % bi, ldx)])
                        if tt >= 1:
                            ln_finish(tt - 1)
                    ln_finish(8)
                    lnxT_all = [("s", max(lnxT_evac))]
                sc.barrier()

                with _ES() as bsc:
                    bb = bsc.enter_context
                    QT = bb(nc.sbuf_tensor("QT%d" % sp, [128, 1024], BF16))
                    KT = bb(nc.sbuf_tensor("KT%d" % sp, [128, HALO + 1024], BF16))
                    qb_t = [bb(nc.sbuf_tensor("qb_t%d_%d" % (sp, i), [128, 512], BF16)) for i in range(5)]
                    qs_t = [bb(nc.sbuf_tensor("qs_t%d_%d" % (sp, i), [128, 512], BF16)) for i in range(5)]
                    qc_t = [bb(nc.sbuf_tensor("qc_t%d_%d" % (sp, i), [128, 512], BF16)) for i in range(5)]
                    rope_free = [[] for _ in range(5)]
                    v4 = [bb(nc.sbuf_tensor("v4_%d_%d" % (sp, i), [128, 9, 512], BF16)) for i in range(2)]
                    wq_b = [bb(nc.sbuf_tensor("wq_b%d_%d" % (sp, i), [128, NK, 128], BF16)) for i in range(2)]
                    wk_b = [bb(nc.sbuf_tensor("wk_b%d_%d" % (sp, i), [128, NK, 128], BF16)) for i in range(2)]
                    wv_b = [bb(nc.sbuf_tensor("wv_b%d_0" % sp, [128, NK, 512], BF16))]
                    Pf = [bb(nc.sbuf_tensor("Pf%d_%d" % (sp, i), [128, 256], F32)) for i in range(8)]
                    Pbf = [[bb(nc.sbuf_tensor("Pbf%d_%d_%d" % (sp, pp, i), [128, 256], BF16))
                            for i in range(8)] for pp in range(2)]
                    PTb = [bb(nc.sbuf_tensor("PT%d_%d" % (sp, i), [128, 2, 128], BF16)) for i in range(8)]
                    smx = bb(nc.sbuf_tensor("smx%d" % sp, [128, 32], F32))
                    wq_free = [[], []]
                    wk_free = [[], []]
                    wv_free = [[]]
                    Pf_free = [[] for _ in range(8)]
                    Pbf_free = [[[] for _ in range(8)] for _ in range(2)]
                    PT_free = [[] for _ in range(8)]
                    v4_ready = [[], []]
                    ptav_free = [([bbank.free_dep[i]] if bbank.free_dep[i] else [])
                                 for i in range(2)]

                    def emit_ptav(ph, pvnrs, pgvi):
                        ppar = ph % 2
                        spts = [None] * 8
                        glast = [None, None]
                        for g in range(2):
                            pbx = pb[g]
                            tp = None
                            for qi in range(4):
                                qt = g * 4 + qi
                                sl = qi * 2
                                for half in range(2):
                                    tp = sc.op("t", lambda e, o=pbx[:, sl + half, :],
                                               i=Pbf[ppar][qt][:, half * 128:(half + 1) * 128], idq=idm_b[:, :]:
                                               e.transpose(o, i, idq),
                                               deps=[("v", pvnrs[qt])] + (ptav_free[g] if qi == 0 and half == 0 else []),
                                               inc=(qi == 3 and half == 1))
                                Pbf_free[ppar][qt] = [("t", tp)]
                            glast[g] = tp
                        for g in range(2):
                            pbx = pb[g]
                            last_spt = None
                            for qi in range(4):
                                qt = g * 4 + qi
                                sl = qi * 2
                                last_spt = sc.op("s", lambda e, o=PTb[qt][:, :, :], i=pbx[:, sl:sl + 2, :]:
                                                 e.copy(o, i), deps=[("t", glast[g])] + PT_free[qt])
                                spts[qt] = last_spt
                            ptav_free[g] = [("s", last_spt)]
                        for qt in range(8):
                            j2, psav, fdep2 = fbank.acquire()
                            hx = ph % 4
                            sc.op("t", lambda e, o=psav[:, :128], l=v4[pgvi][:, qt, hx * 128:(hx + 1) * 128],
                                  r=PTb[qt][:, 0, :]: e.matmul(o, l, r, start=True, stop=False),
                                  deps=v4_ready[pgvi] + [("s", spts[qt])] + ([fdep2] if fdep2 else []), inc=False)
                            ta = sc.op("t", lambda e, o=psav[:, :128], l=v4[pgvi][:, qt + 1, hx * 128:(hx + 1) * 128],
                                       r=PTb[qt][:, 1, :]: e.matmul(o, l, r, start=False, stop=True),
                                       deps=[("s", spts[qt])])
                            PT_free[qt] = [("t", ta)]
                            sae = sc.op("s", lambda e, o=attnT[:, ph, qt * 128:(qt + 1) * 128],
                                        i=psav[:, :128]: e.activation(o, i, AF.Copy), deps=[("t", ta)])
                            fbank.consumed(j2, "s", sae)

                    pend = None
                    for h in range(NH):
                        emit_conv(1)
                        wbi = h % 2
                        ldq = sc.dma("g", "wq%d" % wbi, lambda e, o=wq_b[wbi][:, :, :],
                                     i=wq[:, h * 128:(h + 1) * 128].rearrange("(k p) d -> p k d", p=128):
                                     e.dma_start(o, i), deps=wq_free[wbi])
                        ldk = sc.dma("g", "wk%d" % wbi, lambda e, o=wk_b[wbi][:, :, :],
                                     i=wk[:, h * 128:(h + 1) * 128].rearrange("(k p) d -> p k d", p=128):
                                     e.dma_start(o, i), deps=wk_free[wbi])
                        gvi = 0
                        vpar = (h // 4) % 2
                        if h % 4 == 0:
                            ldv = sc.dma(
                                "g", "wv%d" % gvi, lambda e, o=wv_b[gvi][:, :, :],
                                i=wv[:, h * 128:(h + 4) * 128].rearrange("(k p) d -> p k d", p=128):
                                e.dma_start(o, i), deps=wv_free[gvi])
                            v_ev = []
                            for t9 in range(9):
                                j, psx, fdep = fbank.acquire()
                                mmdeps = lnxT_all + [("wv%d" % gvi, ldv)] + ([fdep] if fdep else [])
                                tl = None
                                for k in range(NK):
                                    tl = sc.op("t", lambda e, o=psx[:, :], l=lnxT[:, k, t9 * 128:(t9 + 1) * 128],
                                               r=wv_b[gvi][:, k, :], st=(k == 0), sp_=(k == NK - 1):
                                               e.matmul(o, l, r, start=st, stop=sp_),
                                               deps=mmdeps, inc=(k == NK - 1))
                                ev = sc.op("s", lambda e, o=v4[vpar][:, t9, :], i=psx[:, :]:
                                           e.activation(o, i, AF.Copy), deps=[("t", tl)])
                                fbank.consumed(j, "s", ev)
                                v_ev.append(ev)
                            v4_ready[vpar] = [("s", max(v_ev))]
                            wv_free[gvi] = [("t", sc.cnt["t"])]

                        chunks = ([(QT, 128, c0, cw, wq_b[wbi], "wq%d" % wbi, ldq)
                                   for (c0, cw) in [(128, 512), (640, 512)]] +
                                  [(KT, 0, c0, cw, wk_b[wbi], "wk%d" % wbi, ldk)
                                   for (c0, cw) in [(0, 512), (512, 512), (1024, 128)]])
                        ses = []
                        for ci, (dst, off0, c0, cw, wbuf, wslot, ldw) in enumerate(chunks):
                            j, psx, fdep = fbank.acquire()
                            mmdeps = lnxT_all + [(wslot, ldw)] + ([fdep] if fdep else [])
                            tl = None
                            for k in range(NK):
                                tl = sc.op("t", lambda e, o=psx[:, :cw], l=wbuf[:, k, :],
                                           r=lnxT[:, k, c0:c0 + cw], st=(k == 0), sp_=(k == NK - 1):
                                           e.matmul(o, l, r, start=st, stop=sp_),
                                           deps=mmdeps, inc=(k == NK - 1))
                            se = sc.op("s", lambda e, o=qb_t[ci][:, :cw], i=psx[:, :cw]:
                                       e.activation(o, i, AF.Copy),
                                       deps=[("t", tl)] + rope_free[ci])
                            fbank.consumed(j, "s", se)
                            ses.append(se)
                        vqs = []
                        for ci, (dst, off0, c0, cw, wbuf, wslot, ldw) in enumerate(chunks):
                            gc0 = base + c0
                            vqs.append(sc.op("v", lambda e, o=qs_t[ci][:, :cw], i0=qb_t[ci][:, :cw],
                                             i1=sin_b[:, gc0:gc0 + cw]:
                                             e.tensor_mul(o, i0, i1),
                                             deps=[("s", ses[ci]), ("gc", ld_const_g)]))
                        vcs = []
                        for ci, (dst, off0, c0, cw, wbuf, wslot, ldw) in enumerate(chunks):
                            gc0 = base + c0
                            vcs.append(sc.op("v", lambda e, o=qc_t[ci][:, :cw], i0=qb_t[ci][:, :cw],
                                             i1=cos_b[:, gc0:gc0 + cw]:
                                             e.tensor_mul(o, i0, i1), deps=[("s", ses[ci])]))
                        trs = []
                        for ci, (dst, off0, c0, cw, wbuf, wslot, ldw) in enumerate(chunks):
                            j2, psr, fdep2 = fbank.acquire()
                            trs.append((j2, psr, sc.op(
                                "t", lambda e, o=psr[:, :cw], l=rt_b[:, :], r=qs_t[ci][:, :cw]:
                                e.matmul(o, l, r, start=True, stop=True),
                                deps=[("v", vqs[ci]), ("yc", ld_const_y)] + ([fdep2] if fdep2 else []))))
                        vfs = []
                        for ci, (dst, off0, c0, cw, wbuf, wslot, ldw) in enumerate(chunks):
                            j2, psr, tr = trs[ci]
                            vf = sc.op("v", lambda e, o=dst[:, c0 - off0:c0 - off0 + cw],
                                       i0=qc_t[ci][:, :cw], i1=psr[:, :cw]:
                                       e.tensor_add(o, i0, i1), deps=[("v", vcs[ci]), ("t", tr)])
                            fbank.consumed(j2, "v", vf)
                            rope_free[ci] = [("v", vf), ("t", tr)]
                            vfs.append(vf)
                        wq_free[wbi] = [("t", sc.cnt["t"])]
                        wk_free[wbi] = [("t", sc.cnt["t"])]
                        qk_ready = [("v", max(vfs))]

                        vms, vmxs, sxps, vsms, vrcs = [], [], [], [], []
                        for qt in range(8):
                            j, psx, fdep = fbank.acquire()
                            ts_ = sc.op("t", lambda e, o=psx[:, :256], l=QT[:, qt * 128:(qt + 1) * 128],
                                        r=KT[:, qt * 128:qt * 128 + 256]:
                                        e.matmul(o, l, r, start=True, stop=True),
                                        deps=qk_ready + ([fdep] if fdep else []))
                            mask = mA_b if (sp == 0 and qt == 0) else mB_b
                            vm = sc.op("v", lambda e, o=Pf[qt][:, :], i=psx[:, :256], m=mask[:, :]:
                                       e.scalar_tensor_tensor(o, i, SCALE, m, ALU.mult, ALU.add),
                                       deps=[("t", ts_), ("yc", ld_const_y)] + Pf_free[qt])
                            fbank.consumed(j, "v", vm)
                            vms.append(vm)
                        for qt in range(8):
                            vmxs.append(sc.op("v", lambda e, o=smx[:, qt * 4:qt * 4 + 1], i=Pf[qt][:, :]:
                                              e.reduce_max(o, i, axis=AX.X, negate=True),
                                              deps=[("v", vms[qt])]))
                        for qt in range(8):
                            sxps.append(sc.op("s", lambda e, o=Pf[qt][:, :], i=Pf[qt][:, :],
                                              b=smx[:, qt * 4:qt * 4 + 1]:
                                              e.activation(o, i, AF.Exp, bias=b),
                                              deps=[("v", vmxs[qt])]))
                        for qt in range(8):
                            vsms.append(sc.op("v", lambda e, o=smx[:, qt * 4 + 1:qt * 4 + 2], i=Pf[qt][:, :]:
                                              e.reduce_sum(o, i, axis=AX.X), deps=[("s", sxps[qt])]))
                        for qt in range(8):
                            vrcs.append(sc.op("v", lambda e, o=smx[:, qt * 4 + 2:qt * 4 + 3],
                                              i=smx[:, qt * 4 + 1:qt * 4 + 2]:
                                              e.reciprocal(o, i), deps=[("v", vsms[qt])]))
                        vnrs = []
                        for qt in range(8):
                            vnr = sc.op("v", lambda e, o=Pbf[h % 2][qt][:, :], i=Pf[qt][:, :],
                                        r=smx[:, qt * 4 + 2:qt * 4 + 3]:
                                        e.tensor_scalar(o, i, r, None, ALU.mult),
                                        deps=[("v", vrcs[qt])] + Pbf_free[h % 2][qt])
                            Pf_free[qt] = [("v", vnr)]
                            vnrs.append(vnr)
                        if pend is not None:
                            emit_ptav(*pend)
                        pend = (h, vnrs, vpar)

                    emit_ptav(*pend)
                    bbank.consumed(0, "s", sc.cnt["s"])
                    bbank.consumed(1, "s", sc.cnt["s"])
                    attnT_all = [("s", sc.cnt["s"])]
                sc.barrier()

                with _ES() as b3sc:
                    b3 = b3sc.enter_context
                    wo_b = [b3(nc.sbuf_tensor("wo_b%d_%d" % (sp, i), [128, NK, 512], BF16)) for i in range(2)]
                    xc_b = [b3(nc.sbuf_tensor("xc_b%d_%d" % (sp, i), [128, 512], F32)) for i in range(2)]
                    x1c = [b3(nc.sbuf_tensor("x1c%d_%d" % (sp, i), [128, 512], F32)) for i in range(2)]
                    x1t = [b3(nc.sbuf_tensor("x1t%d_%d" % (sp, i), [128, D], F32)) for i in range(3)]
                    sqsB = b3(nc.sbuf_tensor("sqB%d" % sp, [128, D], F32))
                    ln2b = [b3(nc.sbuf_tensor("ln2b%d_%d" % (sp, i), [128, D], BF16))
                            for i in range(2)]
                    l2tb = [b3(nc.sbuf_tensor("l2tb%d_%d" % (sp, i), [128, NK, 128], BF16)) for i in range(2)]
                    statsB = {}
                    for nm in ("ssum", "ssq", "mean", "m2", "var", "rstd"):
                        statsB[nm] = [b3(nc.sbuf_tensor("stB%d_%s%d" % (sp, nm, i), [128, 1], F32))
                                      for i in range(3)]
                    gsc = {nm: b3(nc.sbuf_tensor("g%d_%s" % (sp, nm),
                                                 [128, NE if nm in ("lg", "im", "mk", "e1") else 1], F32))
                           for nm in ("lg", "im", "mk", "e1", "nm1", "nm2", "den", "rcd")}
                    gbf = b3(nc.sbuf_tensor("gbf%d" % sp, [128, NE], BF16))
                    wo_free = [[], []]
                    xc_free = [[], []]
                    x1c_free = [[], []]
                    x1t_free = [[], [], []]
                    l2tb_free = [[], []]
                    ln2b_free = [[], []]
                    stB_war = [{"war": []} for _ in range(3)]

                    for dc in range(4):
                        emit_conv(1)
                        wbi = dc % 2
                        ldw = sc.dma("g", "wo%d" % wbi, lambda e, o=wo_b[wbi][:, :, :],
                                     i=wo[:, dc * 512:(dc + 1) * 512].rearrange("(k p) d -> p k d", p=128):
                                     e.dma_start(o, i), deps=wo_free[wbi])
                        for tt in range(8):
                            bi = tt % 2
                            j, psx, fdep = fbank.acquire()
                            mmdeps = attnT_all + [("wo%d" % wbi, ldw), ("gc", ld_const_g),
                                                  ("yc", ld_const_y)] + ([fdep] if fdep else [])
                            for k in range(NK):
                                sc.op("t", lambda e, o=psx[:, :], l=attnT[:, k, tt * 128:(tt + 1) * 128],
                                      r=wo_b[wbi][:, k, :], st=(k == 0):
                                      e.matmul(o, l, r, start=st, stop=False),
                                      deps=mmdeps, inc=False)
                            tl = sc.op("t", lambda e, o=psx[:, :], l=ones_b[:, :],
                                       r=bo_b[:, dc * 512:(dc + 1) * 512]:
                                       e.matmul(o, l, r, start=False, stop=True), deps=mmdeps)
                            r0 = HALO + sp * 1024 + tt * 128
                            ldxc = sc.dma("y", "xc%d" % bi, lambda e, o=xc_b[bi][:, :], i=xh[r0:r0 + 128, dc * 512:(dc + 1) * 512]:
                                          e.dma_start(o, i), deps=xc_free[bi])
                            vx = sc.op("v", lambda e, o=x1c[bi][:, :], i0=psx[:, :], i1=xc_b[bi][:, :]:
                                       e.tensor_add(o, i0, i1),
                                       deps=[("t", tl), ("xc%d" % bi, ldxc)] + x1c_free[bi])
                            fbank.consumed(j, "v", vx)
                            xc_free[bi] = [("v", vx)]
                            rl = sp * 1024 + tt * 128
                            st1 = sc.dma("s", "sx%d" % bi, lambda e, o=x1s[rl:rl + 128, dc * 512:(dc + 1) * 512],
                                         i=x1c[bi][:, :]: e.dma_start(o, i), deps=[("v", vx)])
                            x1c_free[bi] = [("sx%d" % bi, st1)]
                            x1_store[(sp, tt, dc)] = st1
                        wo_free[wbi] = [("t", sc.cnt["t"])]

                    def p2_finish(tt):
                        bi = tt % 3
                        li = tt % 2
                        rl = sp * 1024 + tt * 128
                        gt_g = sp * 8 + tt
                        st = stB_war[bi]
                        s4 = _ln_b(sc, x1t[bi][:, :], st, ln2b[li][:, :],
                                   out_deps=ln2b_free[li])
                        x1t_free[bi] = [("v", s4)]
                        str_ = sc.dma("s", "sr%d" % li, lambda e, o=l2r[rl:rl + 128, :],
                                      i=ln2b[li][:, :]: e.dma_start(o, i), deps=[("v", s4)])
                        evs = []
                        for g2 in range(2):
                            j, pbx, fdep = bbank.acquire()
                            tdeps = [("v", s4)] + ([fdep] if fdep else [])
                            tl = None
                            for k2 in range(8):
                                k = g2 * 8 + k2
                                tl = sc.op("t", lambda e, o=pbx[:, k2, :],
                                           i=ln2b[li][:, k * 128:(k + 1) * 128], idq=idm_b[:, :]:
                                           e.transpose(o, i, idq),
                                           deps=tdeps if k2 == 0 else [("v", s4)],
                                           inc=(k2 == 7))
                            ev = sc.op("s", lambda e, o=l2tb[li][:, g2 * 8:(g2 + 1) * 8, :], i=pbx[:, :, :]:
                                       e.copy(o, i), deps=[("t", tl)] + (l2tb_free[li] if g2 == 0 else []))
                            bbank.consumed(j, "s", ev)
                            evs.append(ev)
                        ln2b_free[li] = [("t", sc.cnt["t"]), ("sr%d" % li, str_)]
                        stl = sc.dma("s", "sl%d" % li, lambda e, o=l2s.rearrange("(k p) t -> p k t", p=128)[:, :, rl:rl + 128],
                                     i=l2tb[li][:, :, :]: e.dma_start(o, i), deps=[("s", max(evs))])
                        j, psx, fdep = fbank.acquire()
                        tl = None
                        for k in range(NK):
                            tl = sc.op("t", lambda e, o=psx[:, :NE], l=l2tb[li][:, k, :],
                                       r=gw_b[:, k, :], st=(k == 0), sp_=(k == NK - 1):
                                       e.matmul(o, l, r, start=st, stop=sp_),
                                       deps=[("s", max(evs)), ("gc", ld_const_g)] + ([fdep] if fdep else []),
                                       inc=(k == NK - 1))
                        l2tb_free[li] = [("t", tl), ("sl%d" % li, stl)]
                        lg = gsc["lg"][:, :]
                        v0 = sc.op("v", lambda e, o=lg, i=psx[:, :NE]: e.tensor_copy(o, i),
                                   deps=[("t", tl), ("v", sc.cnt["v"]), ("s", sc.cnt["s"])])
                        fbank.consumed(j, "v", v0)
                        v1 = sc.op("v", lambda e, o=gsc["nm1"][:, :], i=lg:
                                   e.reduce_max(o, i, axis=AX.X, negate=True), deps=[("v", v0)])
                        v2 = sc.op("v", lambda e, o=gsc["im"][:, :], i=lg, m=gsc["nm1"][:, :]:
                                   e.tensor_scalar(o, i, m, 0.0, ALU.add, ALU.is_ge), deps=[("v", v1)])
                        v3 = sc.op("v", lambda e, o=gsc["mk"][:, :], i0=gsc["im"][:, :], i1=lg:
                                   e.scalar_tensor_tensor(o, i0, NEG, i1, ALU.mult, ALU.add),
                                   deps=[("v", v2)])
                        v4_ = sc.op("v", lambda e, o=gsc["nm2"][:, :], i=gsc["mk"][:, :]:
                                    e.reduce_max(o, i, axis=AX.X, negate=True), deps=[("v", v3)])
                        s1 = sc.op("s", lambda e, o=gsc["e1"][:, :], i=lg, b=gsc["nm1"][:, :]:
                                   e.activation(o, i, AF.Exp, bias=b), deps=[("v", v1), ("v", v3)])
                        s2 = sc.op("s", lambda e, o=gsc["den"][:, :], i=gsc["nm2"][:, :], b=gsc["nm1"][:, :]:
                                   e.activation(o, i, AF.Exp, bias=b, scale=-1.0), deps=[("v", v4_), ("s", s1)])
                        v5 = sc.op("v", lambda e, o=gsc["den"][:, :], i=gsc["den"][:, :]:
                                   e.tensor_scalar(o, i, 1.0, None, ALU.add), deps=[("s", s2)])
                        v6 = sc.op("v", lambda e, o=gsc["rcd"][:, :], i=gsc["den"][:, :]:
                                   e.reciprocal(o, i), deps=[("v", v5)])
                        v7 = sc.op("v", lambda e, o=gsc["mk"][:, :], i=lg, m=gsc["nm2"][:, :]:
                                   e.tensor_scalar(o, i, m, 0.0, ALU.add, ALU.is_ge), deps=[("v", v4_), ("s", s2)])
                        v8 = sc.op("v", lambda e, o=gsc["e1"][:, :], i=gsc["e1"][:, :], m=gsc["rcd"][:, :]:
                                   e.tensor_scalar(o, i, m, None, ALU.mult), deps=[("v", v6), ("s", s1)])
                        v9 = sc.op("v", lambda e, o=gates[:, gt_g * NE:(gt_g + 1) * NE],
                                   i0=gsc["e1"][:, :], i1=gsc["mk"][:, :]:
                                   e.tensor_mul(o, i0, i1), deps=[("v", v7), ("v", v8)])
                        sg = sc.op("s", lambda e, o=gbf[:, :], i=gates[:, gt_g * NE:(gt_g + 1) * NE]:
                                   e.copy(o, i), deps=[("v", v9)])
                        jb, pbx, fdepb = bbank.acquire()
                        tg = sc.op("t", lambda e, o=pbx[:NE, 0, :], i=gbf[:, :], idq=idm_b[:, :]:
                                   e.transpose(o, i, idq), deps=[("s", sg)] + ([fdepb] if fdepb else []))
                        sgt = sc.op("s", lambda e, o=gatesT[:, gt_g * 128:(gt_g + 1) * 128], i=pbx[:NE, 0, :]:
                                    e.copy(o, i), deps=[("t", tg)])
                        bbank.consumed(jb, "s", sgt)

                    for tt in range(8):
                        bi = tt % 3
                        rl = sp * 1024 + tt * 128
                        ldx1 = sc.dma("y", "x1t%d" % bi, lambda e, o=x1t[bi][:, :], i=x1s[rl:rl + 128, :]:
                                      e.dma_start(o, i),
                                      deps=[("sx0", sc.dslot.get("sx0", 0)),
                                            ("sx1", sc.dslot.get("sx1", 0))] + x1t_free[bi])
                        st = stB_war[bi]
                        for nm in statsB:
                            st[nm] = statsB[nm][bi][:, :]
                        st["sq"] = sqsB[:, :]
                        _ln_a(sc, x1t[bi][:, :], st, deps=[("x1t%d" % bi, ldx1)])
                        if tt >= 1:
                            p2_finish(tt - 1)
                    p2_finish(7)
                sc.barrier()

        if debug:
            sc.dma("s", "so", lambda e: e.dma_start(dbg_g[:, :], gates[:, :]))
        sc.flush()


        from concourse.bass_core_rust import IndirectOffsetOnAxis

        abw.close()

        with ExitStack() as c0:
            a = c0.enter_context
            selT = a(nc.sbuf_tensor("selT", [NE, TOK], F32))
            pfx = [a(nc.sbuf_tensor("pfx%d" % i, [NE, TOK], F32)) for i in range(2)]
            mp_tok = a(nc.sbuf_tensor("mp_tok", [128, 16, NE], F32))
            Bt = a(nc.sbuf_tensor("Bt", [128, 16, CAP], F32))
            comb = a(nc.sbuf_tensor("comb", [128, 16, 3], F32))
            idxg = a(nc.sbuf_tensor("idxg", [4, CAP], F32))
            iop1_b = a(nc.sbuf_tensor("iop1_b", [128, CAP], F32))
            tokv_b = a(nc.sbuf_tensor("tokv_b", [128, 16], F32))

            BIGC = 65536.0
            sc.dma("y", "yc2", lambda e: e.dma_start(iop1_b[:, :], iotap1))
            ldc2 = sc.dma("y", "yc2", lambda e: e.dma_start(tokv_b[:, :], tokv))
            v0 = sc.op("v", lambda e: e.tensor_scalar(selT[:, :], gatesT[:, :], 0.0,
                                                      None, ALU.is_gt),
                       deps=[("s", sc.cnt["s"])])
            vp = sc.op("v", lambda e: e.tensor_copy(pfx[0][:, :], selT[:, :]),
                       deps=[("v", v0)])
            cur = 0
            sh = 1
            while sh < TOK:
                nxt = 1 - cur
                vp1 = sc.op("v", lambda e, c=cur, n=nxt, s_=sh:
                            e.tensor_add(pfx[n][:, s_:], pfx[c][:, s_:], pfx[c][:, :TOK - s_]),
                            deps=[("v", vp)])
                vp = sc.op("v", lambda e, c=cur, n=nxt, s_=sh:
                           e.tensor_copy(pfx[n][:, :s_], pfx[c][:, :s_]),
                           deps=[("v", vp1)])
                cur = nxt
                sh *= 2
            vb = sc.op("v", lambda e, c=cur: e.tensor_scalar(pfx[1 - c][:, :], pfx[c][:, :],
                                                             BIGC, None, ALU.add),
                       deps=[("v", vp)])
            vm_ = sc.op("v", lambda e, c=cur: e.scalar_tensor_tensor(
                pfx[c][:, :], selT[:, :], -BIGC, pfx[1 - c][:, :], ALU.mult, ALU.add),
                deps=[("v", vb)])
            j, psx, fdep = fbank.acquire()
            tl = None
            for g in range(16):
                tl = sc.op("t", lambda e, o=psx[:, g * 8:(g + 1) * 8],
                           i=pfx[cur][:, g * 128:(g + 1) * 128], idq=idmf_b[:NE, :NE]:
                           e.transpose(o, i, idq),
                           deps=[("v", vm_)] + ([fdep] if fdep and g == 0 else []),
                           inc=(g == 15))
            sev = sc.op("s", lambda e, i=psx[:, :128].rearrange("p (g e) -> p g e", e=NE):
                        e.copy(mp_tok[:, :, :], i), deps=[("t", tl)])
            fbank.consumed(j, "s", sev)
            ite = []
            for ei in range(NE):
                bts = []
                for g in range(16):
                    bts.append(sc.op("v", lambda e, g=g, ei=ei:
                                     e.tensor_scalar(Bt[:, g, :], iop1_b[:, :],
                                                     mp_tok[:, g, ei:ei + 1], None, ALU.is_equal),
                                     deps=[("s", sev), ("yc2", ldc2),
                                           ("t", sc.cnt["t"]), ("s", sc.cnt["s"])]))
                vcb = sc.op("v", lambda e, ei=ei: e.tensor_copy(comb[:, :, 1:2],
                                                                gates[:, ei::NE].rearrange("p (g o) -> p g o", o=1)),
                            deps=[("v", sc.cnt["v"]), ("t", sc.cnt["t"])])
                vcb2 = sc.op("v", lambda e: e.tensor_copy(comb[:, :, 0:1],
                                                          tokv_b[:, :].rearrange("p (g o) -> p g o", o=1)),
                             deps=[("yc2", ldc2), ("t", sc.cnt["t"])])
                vcb3 = sc.op("v", lambda e: e.memset(comb[:, :, 2:3], 1.0),
                             deps=[("t", sc.cnt["t"])])
                evs2 = []
                for (cs0, csw) in ((0, 512), (512, 128)):
                    j, psx, fdep = fbank.acquire()
                    tl = None
                    for g in range(16):
                        tl = sc.op("t", lambda e, g=g, cs0=cs0, csw=csw,
                                   o=psx[:3, :csw]:
                                   e.matmul(o, comb[:, g, :], Bt[:, g, cs0:cs0 + csw],
                                            start=(g == 0), stop=(g == 15)),
                                   deps=[("v", max(bts)), ("v", vcb), ("v", vcb2), ("v", vcb3)] +
                                        ([fdep] if fdep and g == 0 else []),
                                   inc=(g == 15))
                    ev2 = sc.op("s", lambda e, o=idxg[:3, cs0:cs0 + csw], i=psx[:3, :csw]:
                                e.copy(o, i), deps=[("t", tl)])
                    fbank.consumed(j, "s", ev2)
                    evs2.append(ev2)
                sev2 = max(evs2)
                j2, ps2, fdep2 = fbank.acquire()
                tl2 = None
                for t5 in range(5):
                    tl2 = sc.op("t", lambda e, t5=t5, o=ps2[:, t5 * 4:t5 * 4 + 3]:
                                e.transpose(o, idxg[:3, t5 * 128:(t5 + 1) * 128], idmf_b[:3, :3]),
                                deps=[("s", sev2)] + ([fdep2] if fdep2 and t5 == 0 else []),
                                inc=(t5 == 4))
                sev3 = sc.op("s", lambda e, o=idxT[:, ei * 5:(ei + 1) * 5, :],
                             i=ps2[:, :20].rearrange("p (t c) -> p t c", c=4):
                             e.copy(o, i), deps=[("t", tl2)])
                fbank.consumed(j2, "s", sev3)
                ite.append(sev3)
            vii = sc.op("v", lambda e: e.tensor_copy(idx_i[:, :], idxT[:, :, 0]),
                        deps=[("s", max(ite))])
            idx_ready = [("v", vii), ("s", max(ite))]

        sc.barrier()

        gsb_free = [[], []]
        lrg_free = [[], []]
        bounce_done = {}

        def emit_bounce(ei):
            par = ei % 2
            stores = []
            for t5 in range(5):
                gb = t5 % 2
                gx = sc.dma("g", "gx%d" % gb, lambda e, t5=t5, ei=ei, o=g_sb[gb][:, :]:
                            e.indirect_dma_start(
                                o, None, l2r,
                                IndirectOffsetOnAxis(idx_i[:, ei * 5 + t5:ei * 5 + t5 + 1], 0)),
                            deps=idx_ready + sr_full + gsb_free[gb])
                gr = sc.dma("s", "gr%d%d" % (par, gb), lambda e, t5=t5, par=par, gb=gb,
                            o=l2rg[par, t5 * 128:(t5 + 1) * 128, :], i=g_sb[gb][:, :]:
                            e.dma_start(o, i), deps=[("gx%d" % gb, gx)] + lrg_free[par])
                gsb_free[gb] = [("gr%d%d" % (par, gb), gr)]
                stores.append(("gr%d%d" % (par, gb), gr))
            bounce_done[ei] = stores[-2:]

        l2t_free = []
        with ExitStack() as cs:
            a = cs.enter_context
            l2t = a(nc.sbuf_tensor("l2t", [128, NK, 1024], BF16))
            w1b = [a(nc.sbuf_tensor("w1b%d" % i, [128, NK, 512], BF16)) for i in range(2)]
            w2b = [a(nc.sbuf_tensor("w2b%d" % i, [128, 8, 512], BF16)) for i in range(2)]
            hraw = a(nc.sbuf_tensor("hraw", [128, 8, 1024], BF16))
            acc = a(nc.sbuf_tensor("acc", [128, 8, 2048], F32))
            x1cb = [a(nc.sbuf_tensor("x1cb%d" % i, [128, 512], F32)) for i in range(2)]

            w1_free = [[], []]
            w2_free = [[], []]
            x1cb_free = [[], []]
            w1_i = 0
            w2_i = 0
            sr_full = [("sr0", sc.dslot.get("sr0", 0)), ("sr1", sc.dslot.get("sr1", 0))]
            vms0 = sc.op("v", lambda e: e.memset(g_sb[0][:, :], 0.0), deps=[])
            vms1 = sc.op("v", lambda e: e.memset(g_sb[1][:, :], 0.0), deps=[])
            gsb_free[0] = [("v", vms0)]
            gsb_free[1] = [("v", vms1)]
            emit_bounce(0)
            emit_bounce(1)

            for s in range(2):
                tok0 = s * 1024
                ldl2 = sc.dma("y", "l2t", lambda e, o=l2t[:, :, :],
                              i=l2s.rearrange("(k p) t -> p k t", p=128)[:, :, tok0:tok0 + 1024]:
                              e.dma_start(o, i),
                              deps=[("sl0", sc.dslot.get("sl0", 0)),
                                    ("sl1", sc.dslot.get("sl1", 0))] + l2t_free)
                acc_v = [[0] * 4 for _ in range(8)]
                so_before = sc.dslot.get("so", 0)
                for ei in (8, 9):
                    half = ei - NE
                    w1blocks = [sw1[:, half * 1024:half * 1024 + 512],
                                sw1[:, half * 1024 + 512:half * 1024 + 1024]]
                    w2q = lambda dc, half=half: sw2[half * 1024:(half + 1) * 1024,
                                                    dc * 512:(dc + 1) * 512]
                    bcol = lambda hc, half=half: sb1t_b[:, half * 8 + hc:half * 8 + hc + 1]
                    hdone = {0: [], 1: []}
                    for blk in range(2):
                        wi = w1_i % 2
                        w1_i += 1
                        ldw1 = sc.dma("g", "w1%d" % wi, lambda e, o=w1b[wi][:, :, :],
                                      i=w1blocks[blk].rearrange("(k p) h -> p k h", p=128):
                                      e.dma_start(o, i), deps=w1_free[wi])
                        for hc2 in range(4):
                            hcf = blk * 4 + hc2
                            for tc in range(2):
                                j, psx, fdep = fbank.acquire()
                                mmdeps = [("w1%d" % wi, ldw1), ("l2t", ldl2)] + \
                                         ([fdep] if fdep else [])
                                tl = None
                                for k in range(NK):
                                    tl = sc.op("t", lambda e, o=psx[:, :], l=w1b[wi][:, k, hc2 * 128:(hc2 + 1) * 128],
                                               r=l2t[:, k, tc * 512:(tc + 1) * 512],
                                               st=(k == 0), sp_=(k == NK - 1):
                                               e.matmul(o, l, r, start=st, stop=sp_),
                                               deps=mmdeps, inc=(k == NK - 1))
                                sh = sc.op("s", lambda e, o=hraw[:, hcf, tc * 512:(tc + 1) * 512],
                                           i=psx[:, :], b=bcol(hcf):
                                           e.activation(o, i, AF.Gelu, bias=b), deps=[("t", tl)])
                                fbank.consumed(j, "s", sh)
                                hdone[tc].append(sh)
                        w1_free[wi] = [("t", sc.cnt["t"])]
                    for dc in range(4):
                        wi = w2_i % 2
                        w2_i += 1
                        ldw2 = sc.dma("g", "w2%d" % wi, lambda e, o=w2b[wi][:, :, :],
                                      i=w2q(dc).rearrange("(k p) d -> p k d", p=128):
                                      e.dma_start(o, i), deps=w2_free[wi])
                        for tt in range(8):
                            j, psx, fdep = fbank.acquire()
                            mmdeps = [("s", max(hdone[tt // 4])), ("w2%d" % wi, ldw2),
                                      ("gc", ld_const_g), ("yc", ld_const_y)] + \
                                     ([fdep] if fdep else [])
                            tl = None
                            for k in range(8):
                                last = (k == 7) and ei != 9
                                tl = sc.op("t", lambda e, o=psx[:, :], l=hraw[:, k, tt * 128:(tt + 1) * 128],
                                           r=w2b[wi][:, k, :], st=(k == 0), sp_=last:
                                           e.matmul(o, l, r, start=st, stop=sp_),
                                           deps=mmdeps, inc=last)
                            if ei == 9:
                                gt_g = s * 8 + tt
                                sc.op("t", lambda e, o=psx[:, :], l=gatesT[:, gt_g * 128:(gt_g + 1) * 128],
                                      r=b2_b[:, dc * 512:(dc + 1) * 512]:
                                      e.matmul(o, l, r, start=False, stop=False),
                                      deps=mmdeps, inc=False)
                                tl = sc.op("t", lambda e, o=psx[:, :], l=ones_b[:, :],
                                           r=sb2_b[:, dc * 512:(dc + 1) * 512]:
                                           e.matmul(o, l, r, start=False, stop=True),
                                           deps=mmdeps)
                            rl = tok0 + tt * 128
                            if ei == 8:
                                bi = tt % 2
                                ldxc = sc.dma("y", "xb%d" % bi, lambda e, o=x1cb[bi][:, :],
                                              i=x1s[rl:rl + 128, dc * 512:(dc + 1) * 512]:
                                              e.dma_start(o, i), deps=x1cb_free[bi])
                                in1 = x1cb[bi][:, :]
                                vdeps = [("t", tl), ("xb%d" % bi, ldxc)]
                                if so_before:
                                    vdeps.append(("so", so_before))
                            else:
                                in1 = acc[:, tt, dc * 512:(dc + 1) * 512]
                                vdeps = [("t", tl), ("v", acc_v[tt][dc])]
                            va = sc.op("v", lambda e, o=acc[:, tt, dc * 512:(dc + 1) * 512],
                                       i0=psx[:, :], i1=in1:
                                       e.scalar_tensor_tensor(o, i0, 1.0, i1, ALU.mult, ALU.add),
                                       deps=vdeps)
                            fbank.consumed(j, "v", va)
                            acc_v[tt][dc] = va
                            if ei == 8:
                                x1cb_free[bi] = [("v", va)]
                            if ei == 9:
                                sc.dma("s", "so",
                                       lambda e, o=out[rl:rl + 128, dc * 512:(dc + 1) * 512],
                                       i=acc[:, tt, dc * 512:(dc + 1) * 512]:
                                       e.dma_start(o, i), deps=[("v", va)])
                        w2_free[wi] = [("t", sc.cnt["t"])]
                l2t_free = [("t", sc.cnt["t"])]

        sc.barrier()

        with ExitStack() as c2:
            a = c2.enter_context
            l2gT = [a(nc.sbuf_tensor("l2gT%d" % i, [128, NK, CAP], BF16)) for i in range(2)]
            hrawE = a(nc.sbuf_tensor("hrawE", [128, 8, CAP], BF16))
            out_sb = [a(nc.sbuf_tensor("out_sb%d" % i, [128, 5, 2048], BF16)) for i in range(2)]
            w1b2 = [a(nc.sbuf_tensor("w1c%d" % i, [128, NK, 512], BF16)) for i in range(2)]
            w2b2 = [a(nc.sbuf_tensor("w2c%d" % i, [128, 8, 512], BF16)) for i in range(2)]

            ltt_free = [[], []]
            w1_free2 = [[], []]
            w2_free2 = [[], []]
            out_free = [[[] for _ in range(5)] for _ in range(2)]
            w1_j = 0
            w2_j = 0

            so_full = ("so", sc.dslot.get("so", 0))

            def emit_lt(ei):
                par = ei % 2
                stores = bounce_done[ei]
                lds = []
                for k in range(NK):
                    lt = sc.dma("y", "lt%d" % par, lambda e, k=k, par=par,
                                o=l2gT[par][:, k, :], i=l2rg[par, :, k * 128:(k + 1) * 128]:
                                e.dma_start_transpose(o, i),
                                deps=stores + ltt_free[par])
                    lds.append(lt)
                lrg_free[par] = [("lt%d" % par, lds[-1])]
                return ("lt%d" % par, lds[-1])

            def emit_gather(ei):
                emit_bounce(ei)
                return emit_lt(ei)

            pend_ld = {}
            pend_ld[0] = emit_lt(0)
            pend_ld[1] = emit_lt(1)
            pending_scatter = None
            for ei in range(NE):
                par = ei % 2
                ldlast = pend_ld[ei]
                wi0 = w1_j % 2
                ldw1_0 = sc.dma("y", "wc1%d" % wi0, lambda e, wi0=wi0, ei=ei, o=w1b2[wi0][:, :, :],
                                i=w1s[ei, :, 0:512].rearrange("(k p) h -> p k h", p=128):
                                e.dma_start(o, i),
                                deps=w1_free2[wi0] + [("wcv%d" % ei, 2)])
                if pending_scatter is not None:
                    pending_scatter()
                    pending_scatter = None
                hdone = {0: [], 512: []}
                for blk in range(2):
                    wi = w1_j % 2
                    w1_j += 1
                    if blk == 0:
                        ldw1 = ldw1_0
                    else:
                        ldw1 = sc.dma("y", "wc1%d" % wi, lambda e, wi=wi, ei=ei, blk=blk, o=w1b2[wi][:, :, :],
                                      i=w1s[ei, :, blk * 512:(blk + 1) * 512].rearrange("(k p) h -> p k h", p=128):
                                      e.dma_start(o, i),
                                      deps=w1_free2[wi] + [("wcv%d" % ei, 2)])
                    for hc2 in range(4):
                        hcf = blk * 4 + hc2
                        for (cs0, csw) in ((0, 512), (512, 128)):
                            j, psx, fdep = fbank.acquire()
                            mmdeps = [("wc1%d" % wi, ldw1), ldlast] + ([fdep] if fdep else [])
                            tl = None
                            for k in range(NK):
                                tl = sc.op("t", lambda e, o=psx[:, :csw], l=w1b2[wi][:, k, hc2 * 128:(hc2 + 1) * 128],
                                           r=l2gT[par][:, k, cs0:cs0 + csw],
                                           st=(k == 0), sp_=(k == NK - 1):
                                           e.matmul(o, l, r, start=st, stop=sp_),
                                           deps=mmdeps, inc=(k == NK - 1))
                            sh = sc.op("s", lambda e, o=hrawE[:, hcf, cs0:cs0 + csw],
                                       i=psx[:, :csw], b=b1t_b[:, ei * 8 + hcf:ei * 8 + hcf + 1]:
                                       e.activation(o, i, AF.Gelu, bias=b), deps=[("t", tl)])
                            fbank.consumed(j, "s", sh)
                            hdone[cs0].append(sh)
                    w1_free2[wi] = [("t", sc.cnt["t"])]
                ltt_free[par] = [("t", sc.cnt["t"])]
                tsv = [[0] * 4 for _ in range(5)]
                for dc in range(4):
                    wi = w2_j % 2
                    w2_j += 1
                    ldw2 = sc.dma("y", "wc2%d" % wi, lambda e, o=w2b2[wi][:, :, :],
                                  i=w2s[ei, :, dc * 512:(dc + 1) * 512].rearrange("(k p) d -> p k d", p=128):
                                  e.dma_start(o, i),
                                  deps=w2_free2[wi] + [("wcv%d" % ei, 2)])
                    for t5 in range(5):
                        j, psx, fdep = fbank.acquire()
                        mmdeps = [("s", max(hdone[0 if t5 < 4 else 512])),
                                  ("wc2%d" % wi, ldw2)] + ([fdep] if fdep else [])
                        tl = None
                        for k in range(8):
                            tl = sc.op("t", lambda e, o=psx[:, :], l=hrawE[:, k, t5 * 128:(t5 + 1) * 128],
                                       r=w2b2[wi][:, k, :], st=(k == 0), sp_=(k == 7):
                                       e.matmul(o, l, r, start=st, stop=sp_),
                                       deps=mmdeps, inc=(k == 7))
                        vts = sc.op("v", lambda e, t5=t5, dc=dc,
                                    o=out_sb[par][:, t5, dc * 512:(dc + 1) * 512],
                                    i=psx[:, :], g=idxT[:, ei * 5 + t5, 1:2]:
                                    e.tensor_scalar(o, i, g, None, ALU.mult),
                                    deps=[("t", tl)] + (out_free[par][t5] if dc == 0 else []))
                        fbank.consumed(j, "v", vts)
                        tsv[t5][dc] = vts
                    w2_free2[wi] = [("t", sc.cnt["t"])]
                if ei + 2 < NE:
                    pend_ld[ei + 2] = emit_gather(ei + 2)

                def mk_scatter(ei=ei, tsv=tsv, par=par):
                    for t5 in range(5):
                        sca = sc.dma("g", "sca", lambda e, t5=t5, ei=ei, i=out_sb[par][:, t5, :]:
                                     e.indirect_dma_start(
                                         out, IndirectOffsetOnAxis(idx_i[:, ei * 5 + t5:ei * 5 + t5 + 1], 0),
                                         i, None, compute_op=ALU.add),
                                     deps=[("v", max(tsv[t5])), so_full,
                                           ("sca", sc.dslot.get("sca", 0))])
                        out_free[par][t5] = [("sca", sca)]
                pending_scatter = mk_scatter
            pending_scatter()

        sc.wait_only("y", [("so", sc.dslot.get("so", 0)),
                           ("sca", sc.dslot.get("sca", 0))])
        sc.wait_only("s", [("sca", sc.dslot.get("sca", 0))])
        sc.flush()

    return nc


def _host_inputs(inputs):
    x = np.ascontiguousarray(inputs["L0_attn_x"], dtype=np.float32)
    wqkv = np.asarray(inputs["L0_attn_W_qkv"], dtype=np.float32)
    shared = {
        "wq": np.ascontiguousarray(wqkv[:, 0::3]),
        "wk": np.ascontiguousarray(wqkv[:, 1::3]),
        "wv": np.ascontiguousarray(wqkv[:, 2::3]),
        "wo": np.ascontiguousarray(inputs["L0_attn_W_out"], dtype=np.float32),
        "bo": np.ascontiguousarray(inputs["L0_attn_b_out"], dtype=np.float32).reshape(1, D),
        "gw": np.ascontiguousarray(inputs["L0_ffn_gate_w"], dtype=np.float32),
        "b1t": np.ascontiguousarray(
            np.asarray(inputs["L0_ffn_b1"], dtype=np.float32).reshape(NE, 8, 128)
            .transpose(2, 0, 1).reshape(128, NE * 8)),
        "b2": np.ascontiguousarray(inputs["L0_ffn_b2"], dtype=np.float32),
        "w1": np.ascontiguousarray(inputs["L0_ffn_w1"], dtype=np.float32),
        "w2": np.ascontiguousarray(inputs["L0_ffn_w2"], dtype=np.float32),
        "sw1": np.ascontiguousarray(inputs["L0_ffn_shared_w1"], dtype=np.float32),
        "sb1t": np.ascontiguousarray(
            np.asarray(inputs["L0_ffn_shared_b1"], dtype=np.float32).reshape(16, 128).T),
        "sw2": np.ascontiguousarray(inputs["L0_ffn_shared_w2"], dtype=np.float32),
        "sb2": np.ascontiguousarray(
            inputs["L0_ffn_shared_b2"], dtype=np.float32).reshape(1, D),
        "rt": _rot_matrix(),
        "iotap1": np.broadcast_to(np.arange(1, CAP + 1, dtype=np.float32)[None, :],
                                  (128, CAP)).copy(),
        "tokv": np.ascontiguousarray(
            (np.arange(16, dtype=np.float32)[None, :] * 128
             + np.arange(128, dtype=np.float32)[:, None])),
        "idm": np.eye(128, dtype=ml_dtypes.bfloat16),
        "idmf": np.eye(128, dtype=np.float32),
        "onesb": np.ones((1, 128), dtype=ml_dtypes.bfloat16),
        "maskB": _mask_b(),
    }
    cores = []
    for c in range(8):
        b, r0 = c // 4, (c % 4) * TOK
        halo = (np.zeros((HALO, D), np.float32) if r0 == 0
                else x[b, r0 - HALO:r0])
        xh = np.ascontiguousarray(np.concatenate([halo, x[b, r0:r0 + TOK]], axis=0))
        pos = np.clip(np.arange(r0 - HALO, r0 + TOK), 0, None).astype(np.float64)
        inv = 1.0 / (10000.0 ** (np.arange(0, HD, 2, dtype=np.float64) / HD))
        ang = np.repeat(pos[:, None] * inv[None, :], 2, axis=1)
        core = dict(shared)
        core["xh"] = xh
        core["cost"] = np.ascontiguousarray(np.cos(ang).T.astype(np.float32))
        core["sint"] = np.ascontiguousarray(np.sin(ang).T.astype(np.float32))
        core["maskA"] = _mask_a() if r0 == 0 else shared["maskB"]
        cores.append(core)
    return cores


def _rot_matrix():
    r = np.zeros((128, 128), np.float32)
    for k in range(64):
        r[2 * k, 2 * k + 1] = -1.0
        r[2 * k + 1, 2 * k] = 1.0
    return r.T.astype(ml_dtypes.bfloat16)


def _mask_b():
    m = np.full((128, 256), NEG, np.float32)
    r = np.arange(128)
    m[:, :128][r[:, None] <= np.arange(128)[None, :]] = 0.0
    m[:, 128:][r[:, None] >= np.arange(128)[None, :]] = 0.0
    return m


def _mask_a():
    m = _mask_b().copy()
    m[:, :128] = NEG
    return m


def kernel(**inputs):
    cores = _host_inputs(inputs)
    nc = build_program()
    res = run_bass_kernel_spmd(nc, cores, core_ids=list(range(8)))
    outp = np.empty((2, 8192, D), np.float32)
    for c in range(8):
        b, r0 = c // 4, (c % 4) * TOK
        outp[b, r0:r0 + TOK] = res.results[c]["out"]
    return outp
